# version 17
# baseline (speedup 1.0000x reference)
import numpy as np
import ml_dtypes
import concourse.bass as bass
import concourse.mybir as mybir
from concourse.bass_utils import run_bass_kernel_spmd

F32 = mybir.dt.float32
BF16 = mybir.dt.bfloat16
AF = mybir.ActivationFunctionType
ALU = mybir.AluOpType
AX = mybir.AxisListType

B, S, D = 4, 4096, 2048
FI = 8192
NCORES = 8
TOK = 2048
NT_ALL = 32
NT_OWN = 16
CHUNK = 1024
EPS = 1e-5
KB = 1024


class Tk:
    __slots__ = ("sem", "val", "stream", "compute")

    def __init__(self, sem, val, stream, compute):
        self.sem, self.val, self.stream, self.compute = sem, val, stream, compute


class Res:
    __slots__ = ("w", "rs")

    def __init__(self):
        self.w = None
        self.rs = []


class Plan:
    STREAMS = ("t", "v", "s", "g", "y")
    CSEM = {"t": "t", "v": "v", "s": "s", "g": "gc"}

    def __init__(self):
        self.ops = {st: [] for st in self.STREAMS}
        self.cnt = {"t": 0, "v": 0, "s": 0, "gc": 0}
        self.dma_slots = []

    def _do(self, stream, sem, amt, emit, reads, writes, waits):
        need = list(waits)
        for r in reads:
            if r.w is not None:
                need.append(r.w)
        for w in writes:
            if w.w is not None:
                need.append(w.w)
            need.extend(w.rs)
        if sem not in self.cnt:
            self.cnt[sem] = 0
            self.dma_slots.append(sem)
        self.cnt[sem] += amt
        tok = Tk(sem, self.cnt[sem], stream, amt == 1)
        self.ops[stream].append((need, emit, sem, amt))
        for r in reads:
            r.rs.append(tok)
        for w in writes:
            w.w = tok
            w.rs = []
        return tok

    def op(self, stream, emit, reads=(), writes=(), waits=()):
        return self._do(stream, self.CSEM[stream], 1, emit, reads, writes, waits)

    def dma(self, stream, slot, emit, reads=(), writes=(), waits=()):
        assert stream in ("g", "y")
        return self._do(stream, "q_" + slot, 16, emit, reads, writes, waits)

    def barrier(self):
        toks = []
        for sem, val in self.cnt.items():
            if sem in ("t", "v", "s", "gc"):
                st = {"t": "t", "v": "v", "s": "s", "gc": "g"}[sem]
                toks.append(Tk(sem, val, st, True))
            else:
                toks.append(Tk(sem, val, "?", False))
        for st in self.STREAMS:
            need = [t for t in toks if not (t.stream == st and t.compute)]
            self.ops[st].append((need, None, None, 0))

    def replay(self, nc, semobjs):
        engines = {"t": nc.tensor, "v": nc.vector, "s": nc.scalar,
                   "g": nc.gpsimd, "y": nc.sync}
        block_ctx = nc.Block()
        with block_ctx as block:
            def mk(st):
                def run(eng):
                    hwm = {}
                    for need, emit, sem, amt in self.ops[st]:
                        for w in need:
                            if st == "t" and w.stream == "t" and w.compute:
                                continue
                            if hwm.get(w.sem, 0) >= w.val:
                                continue
                            eng.wait_ge(semobjs[w.sem], w.val)
                            hwm[w.sem] = w.val
                        if emit is not None:
                            ins = emit(eng)
                            ins.then_inc(semobjs[sem], amt)
                return run
            block.tensor(mk("t"))
            block.vector(mk("v"))
            block.scalar(mk("s"))
            block.gpsimd(mk("g"))
            block.sync(mk("y"))


def build_program(dbg=False, skip_ffn_accum=False):
    nc = bass.Bass("TRN2", target_bir_lowering=False, debug=False,
                   detect_race_conditions=False)

    xarr = nc.dram_tensor("xarr", [S, D], F32, kind="ExternalInput").ap()
    crow = nc.dram_tensor("crow", [1, D], F32, kind="ExternalInput").ap()
    wmod = nc.dram_tensor("wmod", [D, 6 * D], F32, kind="ExternalInput").ap()
    wgP = nc.dram_tensor("wgP", [8, 2, 4, 128, 16, 128], F32, kind="ExternalInput").ap()
    wuP = nc.dram_tensor("wuP", [8, 2, 4, 128, 16, 128], F32, kind="ExternalInput").ap()
    wdP = nc.dram_tensor("wdP", [8, 2, 4, 128, D], F32, kind="ExternalInput").ap()
    ident = nc.dram_tensor("ident", [128, 128], BF16, kind="ExternalInput").ap()
    yout = nc.dram_tensor("yout", [TOK, D], F32, kind="ExternalOutput").ap()

    if dbg:
        dbg_mod = nc.dram_tensor("dbg_mod", [6, D], F32, kind="ExternalOutput").ap()
        dbg_H = nc.dram_tensor("dbg_H", [2, 128, D], F32, kind="ExternalOutput").ap()
        dbg_A = nc.dram_tensor("dbg_A", [2, 128, D], F32, kind="ExternalOutput").ap()
        dbg_hs = nc.dram_tensor("dbg_hs", [8, 128, CHUNK], F32, kind="ExternalOutput").ap()
        dbg2 = nc.dram_tensor("dbg2", [4, 128, D], F32, kind="ExternalOutput").ap()

    scrow = nc.dram_tensor("scrow", [1, D], BF16, kind="Internal").ap()
    modscr = nc.dram_tensor("modscr", [6, D], F32, kind="Internal").ap()
    atscr = nc.dram_tensor("atscr", [128, 16, 16, 128], BF16, kind="Internal").ap()

    BASE = (nc.sbuf_base + 511) // 512 * 512
    TOPB = nc.sbuf_top
    _sz = {F32: 4, BF16: 2}
    cur = [BASE]

    def A(name, shape, dt, at=None):
        nbytes = int(np.prod(shape[1:])) * _sz[dt]
        if at is None:
            at = cur[0]
            cur[0] = at + (nbytes + 255) // 256 * 256
        assert at + nbytes <= TOPB, (name, hex(at), nbytes, TOPB)
        return nc.alloc_sbuf_tensor_at(name, shape, dt, offset=at), at

    id_sb, _ = A("id_sb", [128, 128], BF16)
    sc_col, _ = A("sc_col", [128, 16], BF16)
    ga_col, _ = A("ga_col", [128, 16], F32)
    stats, _ = A("stats", [128, 128], F32)
    PHASE0 = cur[0]

    Hon, _ = A("Hon", [128, 16, D], BF16)
    Hot, o_Hot = A("Hot", [128, 16, D], BF16)
    camul, o_camul = A("camul", [128, D], F32)
    sarep, _ = A("sarep", [128, D], F32)
    xbuf = [A(f"xbuf{i}", [128, D], F32)[0] for i in range(3)]
    t1f, o_t1f = A("t1f", [128, D], F32)
    wmodbuf0, o_wmb = A("wmodbuf0", [128, D // 2], BF16)
    wmodbuf1, _ = A("wmodbuf1", [128, D // 2], BF16)
    wmodbufW = None
    A_sb, o_Asb = A("A_sb", [128, D], BF16)
    o_A2 = A("A2pad", [128, D], BF16)[1]
    wmodbuf = [A("wmbW0", [128, D], BF16, at=o_wmb)[0],
               A("wmbW1", [128, D], BF16, at=o_A2)[0]]
    modrow, o_modrow = A("modrow", [1, D], F32)
    modrow2, _ = A("modrow2", [1, D], F32)
    crow_sb, _ = A("crow_sb", [1, D], F32, at=o_Asb)
    scsil, _ = A("scsil", [1, D], BF16, at=o_modrow)
    ATch2 = [A(f"ATch{i}", [128, 16, 128], BF16, at=o_camul + i * 4096)[0]
             for i in range(2)]
    HT, _ = A("HT", [128, 16, D], BF16, at=o_Hot)
    ATbuf = [A(f"ATbuf{i}", [128, 16, 512], BF16, at=o_camul + i * 16384)[0]
             for i in range(2)]
    xtile = [A(f"xtile{i}", [128, 512], F32, at=o_t1f + i * 2048)[0]
             for i in range(2)]
    x2tile = [A(f"x2tile{i}", [128, 512], F32, at=o_t1f + 4096 + i * 2048)[0]
              for i in range(2)]

    cur[0] = PHASE0
    h2T, _ = A("h2T", [128, 16, CHUNK], BF16)
    hs2 = [A(f"hs{i}", [128, 8, CHUNK], BF16)[0] for i in range(2)]
    wd_sb, _ = A("wd_sb", [128, 8, D], BF16)
    wg_sb = [A(f"wg_sb{i}", [128, 16, 128], BF16)[0] for i in range(4)]
    wu_sb = [A(f"wu_sb{i}", [128, 16, 128], BF16)[0] for i in range(4)]
    hf = []
    o_hf0 = cur[0]
    for i in range(4):
        hf.append(A(f"hf{i}", [128, CHUNK], F32)[0])
    Uf, _ = A("Uf", [128, CHUNK], F32)
    o_ln2 = cur[0]
    af = [A(f"af{i}", [128, CHUNK], F32)[0] for i in range(4)]
    bcmp = [A(f"bcmp{i}", [128, CHUNK], BF16)[0] for i in range(6)]
    keep = [A(f"keep{i}", [128, CHUNK], BF16)[0] for i in range(4)]
    msc = [A(f"msc{i}", [128, CHUNK], BF16)[0] for i in range(2)]
    yt0, o_yt0 = A("ytile0", [128, D], BF16)
    yt1, o_yt1 = A("ytile1", [128, D], BF16)
    ytile = [yt0, yt1]
    gbrep, o_gbrep = A("gbrep", [128, D], F32)
    ln2scr, _ = A("ln2scr", [128, D], F32, at=o_gbrep)
    x2buf = [A(f"x2buf{i}", [128, D], F32, at=o_ln2 + i * 8192)[0] for i in range(3)]
    n2t1, _ = A("n2t1", [128, D], F32, at=o_ln2 + 24576)
    cbmul, _ = A("cbmul", [128, D], F32, at=o_ln2 + 32768)
    sbrep, _ = A("sbrep", [128, D], F32, at=o_ln2 + 40960)
    n2bf, _ = A("n2bf", [128, D], BF16, at=o_hf0)

    ps = [nc.alloc_psum_tensor(f"ps{i}", [128, D], F32) for i in range(2)]
    tp = [ps[1][:, 0:256].bitcast(BF16).rearrange("p (a b) -> p a b", a=4),
          ps[1][:, 512:768].bitcast(BF16).rearrange("p (a b) -> p a b", a=4)]
    tpdum = [ps[1][:, 256:258], ps[1][:, 768:770]]

    def fence_tp(e, tpi):
        return e.matmul(tpdum[tpi], id_sb[0:1, 0:128], id_sb[0:1, 0:2],
                        start=True, stop=True)

    p = Plan()

    def scol(setidx, q):
        c = setidx * 30 + q
        return stats[:, c:c + 1]

    def scols(setidx, q, w):
        c = setidx * 30 + q
        return stats[:, c:c + w]

    r_stats = [[Res() for _ in range(8)] for _ in range(4)]
    r_x = [Res() for _ in range(4)]
    r_H = Res()
    r_mod = Res()
    r_mod2 = Res()
    r_modscr = Res()
    r_scrow = Res()
    r_sccol = Res()
    r_gacol = Res()
    r_camul, r_sarep, r_t1 = Res(), Res(), Res()
    r_wmod = [Res(), Res()]
    r_crow, r_scsil = Res(), Res()
    r_id = Res()
    r_ps0, r_ps1 = Res(), Res()
    r_tp = [Res(), Res()]
    r_Asb = Res()
    r_ATch2 = [Res(), Res()]
    r_atscr = Res()
    r_zc = [Res() for _ in range(4)]

    zZ, zZi, zgaZ, znegm = (stats[:, 120 + i:121 + i] for i in range(4))

    p.dma("y", "ident", lambda e: e.dma_start(out=id_sb[:], in_=ident[:, :]),
          writes=[r_id])
    p.dma("y", "crow", lambda e: e.dma_start(out=crow_sb[:], in_=crow[:, :]),
          writes=[r_crow])
    p.op("s", lambda e: e.activation(scsil[:], crow_sb[:], AF.Silu),
         reads=[r_crow], writes=[r_scsil])
    p.dma("g", "scrow", lambda e: e.dma_start(out=scrow[:, :], in_=scsil[:, :]),
          reads=[r_scsil], writes=[r_scrow])
    for k in range(16):
        p.dma("g", "sccol", lambda e, k=k: e.dma_start(
            out=sc_col[:, k:k + 1],
            in_=scrow[0, 128 * k:128 * (k + 1)].rearrange("(p o) -> p o", o=1)),
            reads=[r_scrow], writes=[r_sccol])

    r_modps = Res()
    modps = [ps[1][0:1, 1024:1536], ps[1][0:1, 1536:2048]]
    sgnrow = t1f[0:1, :]

    def mod_mms_wide(sc, pst, wtoks):
        for k in range(16):
            par = k % 2
            p.dma("g", f"wm{par}", lambda e, k=k, par=par, sc=sc: e.dma_start(
                out=wmodbuf[par][:],
                in_=wmod[128 * k:128 * (k + 1), D * sc:D * (sc + 1)]),
                writes=[r_wmod[par]])

            def mm_mod(e, k=k, par=par, pst=pst):
                for n in range(4):
                    ins = e.matmul(pst[0:1, 512 * n:512 * (n + 1)],
                                   sc_col[:, k:k + 1],
                                   wmodbuf[par][:, 512 * n:512 * (n + 1)],
                                   start=(k == 0), stop=(k == 15))
                return ins
            p.op("t", mm_mod, reads=[r_sccol, r_wmod[par]], writes=wtoks)

    def mod_copy_wide(pst, rtoks):
        p.op("s", lambda e, pst=pst: e.activation(modrow[:], pst[0:1, :], AF.Identity),
             reads=rtoks, writes=[r_mod])

    def mod_mms_narrow(sc):
        for h in range(2):
            for k in range(16):
                par = k % 2
                c0 = D * sc + 1024 * h
                p.dma("g", f"wm{par}", lambda e, k=k, par=par, c0=c0: e.dma_start(
                    out=wmodbuf[par][:, 0:1024],
                    in_=wmod[128 * k:128 * (k + 1), c0:c0 + 1024]),
                    writes=[r_wmod[par]])

                def mm_mod(e, k=k, par=par):
                    for n in range(2):
                        ins = e.matmul(modps[n],
                                       sc_col[:, k:k + 1],
                                       wmodbuf[par][:, 512 * n:512 * (n + 1)],
                                       start=(k == 0), stop=(k == 15))
                    return ins
                p.op("t", mm_mod, reads=[r_sccol, r_wmod[par]], writes=[r_modps])
            p.op("s", lambda e, h=h: e.activation(
                modrow[0:1, 1024 * h:1024 * (h + 1)], ps[1][0:1, 1024:2048],
                AF.Identity),
                reads=[r_modps], writes=[r_mod])

    def mod_post(sc):
        if sc in (1, 4):
            p.op("v", lambda e: e.tensor_scalar(modrow2[:], modrow[:], 1.0, None, ALU.add),
                 reads=[r_mod], writes=[r_mod2])
            src_, r_src = modrow2, r_mod2
        elif sc in (2, 5):
            p.op("v", lambda e: e.tensor_scalar(
                modrow2[:], modrow[:], 0.0, 2.0, ALU.is_ge, ALU.mult),
                reads=[r_mod], writes=[r_mod2])
            p.op("v", lambda e: e.tensor_scalar(
                sgnrow, modrow2[:], -1.0, None, ALU.add),
                reads=[r_mod2], writes=[r_t1])
            p.op("v", lambda e: e.scalar_tensor_tensor(
                modrow2[:], modrow[:], -1.0, modrow[:], ALU.mult, ALU.max),
                reads=[r_mod], writes=[r_mod2])
            p.op("v", lambda e: e.scalar_tensor_tensor(
                modrow[:], modrow2[:], 0.25, sgnrow, ALU.max, ALU.mult),
                reads=[r_mod2, r_t1], writes=[r_mod])
            src_, r_src = modrow, r_mod
        else:
            src_, r_src = modrow, r_mod
        p.dma("g", "modw", lambda e, sc=sc, src_=src_: e.dma_start(
            out=modscr[sc:sc + 1, :], in_=src_[:, :]),
            reads=[r_src], writes=[r_modscr])

    def mod_chunk(sc):
        mod_mms_narrow(sc)
        mod_post(sc)

    mod_mms_wide(1, ps[0], [r_ps0])
    mod_mms_wide(0, ps[1], [r_modps, r_tp[0], r_tp[1]])

    def modulate(ct):
        Hsl = (Hon if ct < 16 else Hot)[:, ct % 16, :]
        p.op("v", lambda e, Hsl=Hsl: e.tensor_tensor(t1f[:], Hsl, camul[:], ALU.mult),
             reads=[r_H, r_camul], writes=[r_t1])
        p.op("v", lambda e, Hsl=Hsl: e.tensor_tensor(Hsl, t1f[:], sarep[:], ALU.add),
             reads=[r_t1, r_sarep], writes=[r_H])

    for ct in range(NT_ALL):
        par = ct % 3
        Hdst = (Hon if ct < 16 else Hot)[:, ct % 16, :]
        rs = r_stats[par]
        p.dma("y", f"xb{par}", lambda e, ct=ct, par=par: e.dma_start(
            out=xbuf[par][:], in_=xarr[128 * ct:128 * (ct + 1), :]),
            writes=[r_x[par]])
        for q4 in range(4):
            p.op("v", lambda e, par=par, q4=q4: e.bn_stats(
                scols(par, 6 * q4, 6), xbuf[par][:, 512 * q4:512 * (q4 + 1)]),
                reads=[r_x[par]], writes=[rs[0]])
        p.op("v", lambda e, par=par: e.bn_aggr(scols(par, 24, 2), scols(par, 0, 24)),
             reads=[rs[0]], writes=[rs[1]])
        p.op("v", lambda e, par=par: e.tensor_scalar(
            scol(par, 26), scol(par, 25), 1.0, EPS, ALU.mult, ALU.add),
            reads=[rs[1]], writes=[rs[2]])
        p.op("s", lambda e, par=par: e.activation(scol(par, 27), scol(par, 26), AF.Ln),
             reads=[rs[2]], writes=[rs[3]])
        p.op("s", lambda e, par=par: e.activation(scol(par, 28), scol(par, 27), AF.Exp, scale=-0.5),
             reads=[rs[3]], writes=[rs[4]])
        p.op("v", lambda e, par=par: e.scalar_tensor_tensor(
            scol(par, 29), scol(par, 24), -1.0, scol(par, 28), ALU.mult, ALU.mult),
            reads=[rs[1], rs[4]], writes=[rs[5]])
        p.op("s", lambda e, ct=ct, par=par, Hdst=Hdst: e.activation(
            Hdst, xbuf[par][:], AF.Identity, bias=scol(par, 29), scale=scol(par, 28)),
            reads=[r_x[par], rs[4], rs[5]], writes=[r_H])

    mod_copy_wide(ps[0], [r_ps0])
    mod_post(1)
    p.dma("g", "bca", lambda e: e.dma_start(out=camul[:], in_=modscr[1:2, :].broadcast_to([128, D])),
          reads=[r_modscr], writes=[r_camul])
    mod_copy_wide(ps[1], [r_modps, r_tp[0], r_tp[1]])
    mod_post(0)
    p.dma("g", "bcs", lambda e: e.dma_start(out=sarep[:], in_=modscr[0:1, :].broadcast_to([128, D])),
          reads=[r_modscr], writes=[r_sarep])
    for ct in range(NT_ALL):
        modulate(ct)

    mod_mms_wide(2, ps[0], [r_ps0])
    mod_copy_wide(ps[0], [r_ps0])
    mod_post(2)
    for k in range(16):
        p.dma("g", "gacol", lambda e, k=k: e.dma_start(
            out=ga_col[:, k:k + 1],
            in_=modscr[2, 128 * k:128 * (k + 1)].rearrange("(p o) -> p o", o=1)),
            reads=[r_modscr], writes=[r_gacol])

    if dbg:
        p.dma("g", "dbg", lambda e: e.dma_start(out=dbg_mod[:, :], in_=modscr[:, :]),
              reads=[r_modscr])
        p.dma("g", "dbg", lambda e: e.dma_start(out=dbg2[0], in_=Hon[:, 0, :]), reads=[r_H])
        p.dma("g", "dbg", lambda e: e.dma_start(out=dbg2[1], in_=camul[:]), reads=[r_camul])
        p.dma("g", "dbg", lambda e: e.dma_start(out=dbg2[2], in_=sarep[:]), reads=[r_sarep])
        p.dma("g", "dbg", lambda e: e.dma_start(out=dbg2[3][:, 0:16], in_=ga_col[:, :]),
              reads=[r_gacol])
        p.dma("g", "dbg", lambda e: e.dma_start(out=dbg2[3][:, 32:64], in_=stats[:, :]),
              reads=[r_stats[0][5], r_stats[1][5]])

    if dbg:
        p.dma("g", "dbg", lambda e: e.dma_start(out=dbg_H[0], in_=Hon[:, 0, :]), reads=[r_H])
        p.dma("g", "dbg", lambda e: e.dma_start(out=dbg_H[1], in_=Hot[:, 0, :]), reads=[r_H])

    def Hsl(k):
        return (Hon if k < 16 else Hot)[:, k % 16, :]

    pend_tp = None
    for bi in range(16):
        def mm_E(e, bi=bi):
            for k in range(NT_ALL):
                h = Hsl(k)
                for n in range(4):
                    ins = e.matmul(ps[0][:, 512 * n:512 * (n + 1)],
                                   h[:, 128 * bi:128 * (bi + 1)],
                                   h[:, 512 * n:512 * (n + 1)],
                                   start=(k == 0), stop=(k == NT_ALL - 1))
            return ins
        p.op("t", mm_E, reads=[r_H], writes=[r_ps0])
        if bi == 3:
            mod_chunk(3)
        elif bi == 6:
            mod_chunk(4)
        elif bi == 9:
            mod_chunk(5)

        if pend_tp is not None:
            pend_tp()
            pend_tp = None

        p.op("v", lambda e: e.reduce_max(znegm, ps[0][:], axis=AX.X, negate=True),
             reads=[r_ps0], writes=[r_zc[3]])
        p.op("s", lambda e: e.activation(A_sb[:], ps[0][:], AF.Exp,
                                         bias=znegm),
             reads=[r_ps0, r_zc[3]], writes=[r_Asb])
        p.op("v", lambda e: e.reduce_sum(zZ, A_sb[:], axis=AX.X),
             reads=[r_Asb], writes=[r_zc[0]])
        p.op("v", lambda e: e.reciprocal(zZi, zZ),
             reads=[r_zc[0]], writes=[r_zc[1]])
        p.op("v", lambda e, bi=bi: e.tensor_tensor(zgaZ, zZi, ga_col[:, bi:bi + 1], ALU.mult),
             reads=[r_zc[1], r_gacol], writes=[r_zc[2]])
        p.op("v", lambda e: e.tensor_scalar(A_sb[:], A_sb[:], zgaZ, None, ALU.mult),
             reads=[r_Asb, r_zc[2]], writes=[r_Asb])
        if dbg and bi == 0:
            p.dma("g", "dbg", lambda e: e.dma_start(out=dbg_A[1], in_=A_sb[:]), reads=[r_Asb])

        def make_tp(bi=bi):
            def emit_all():
                ach = ATch2[bi % 2]
                rach = r_ATch2[bi % 2]
                for g in range(4):
                    tpi = g % 2

                    def mm_tp(e, g=g, tpi=tpi, bi=bi):
                        for i in range(4):
                            e.transpose(tp[tpi][:, i, :],
                                        A_sb[:, 128 * (4 * g + i):128 * (4 * g + i + 1)],
                                        id_sb[:])
                        return fence_tp(e, tpi)
                    p.op("t", mm_tp, reads=[r_Asb, r_id], writes=[r_tp[tpi]])
                    p.op("v", lambda e, g=g, tpi=tpi, ach=ach, rach=rach: e.tensor_copy(
                        ach[:, 4 * g:4 * g + 4, :],
                        tp[tpi][:, :, :]),
                        reads=[r_tp[tpi]], writes=[rach, r_camul, r_sarep])
                p.dma("y", "atsp", lambda e, bi=bi, ach=ach: e.dma_start(
                    out=atscr[:, :, bi, :], in_=ach[:, :, :]),
                    reads=[rach], writes=[r_atscr])
            return emit_all
        pend_tp = make_tp()
        if bi == 15:
            pend_tp()
            pend_tp = None

    p.barrier()

    r_HT = Res()
    for cm in range(16):
        for g in range(4):
            tpi = g % 2

            def mm_ht(e, cm=cm, g=g, tpi=tpi):
                for i in range(4):
                    jk = 4 * g + i
                    e.transpose(tp[tpi][:, i, :],
                                Hon[:, cm, 128 * jk:128 * (jk + 1)],
                                id_sb[:])
                return fence_tp(e, tpi)
            p.op("t", mm_ht, reads=[r_H, r_id], writes=[r_tp[tpi]])
            p.op("v", lambda e, cm=cm, g=g, tpi=tpi: e.tensor_copy(
                HT[:, 4 * g:4 * g + 4, 128 * cm:128 * cm + 128],
                tp[tpi][:, :, :]),
                reads=[r_tp[tpi]], writes=[r_HT])

    p.barrier()

    r_atbuf = [Res(), Res()]
    r_xt = [Res(), Res()]
    r_x2t = [Res(), Res()]
    r_psB = [Res(), Res()]
    r_out = [Res() for _ in range(16)]
    def emit_atb(ic):
        abuf = ic % 2
        p.dma("y", f"atb{abuf}", lambda e, ic=ic, abuf=abuf: e.dma_start(
            out=ATbuf[abuf][:, :, :],
            in_=atscr[:, :, 4 * ic:4 * (ic + 1), :].rearrange("p jk b c -> p jk (b c)")),
            reads=[r_atscr], writes=[r_atbuf[abuf]])

    emit_atb(0)
    for ic in range(4):
        abuf = ic % 2
        for cm in range(16):
            par = cm % 2
            psB = ps[1][:, 512 * par:512 * (par + 1)]
            p.dma("y", f"xt{par}", lambda e, ic=ic, cm=cm, par=par: e.dma_start(
                out=xtile[par][:],
                in_=xarr[128 * cm:128 * (cm + 1), 512 * ic:512 * (ic + 1)]),
                writes=[r_xt[par]])

            def mm_out(e, ic=ic, cm=cm, par=par, abuf=abuf, psB=psB):
                for jk in range(16):
                    ins = e.matmul(psB, HT[:, jk, 128 * cm:128 * (cm + 1)],
                                   ATbuf[abuf][:, jk, :],
                                   start=(jk == 0), stop=(jk == 15))
                return ins
            p.op("t", mm_out, reads=[r_HT, r_atbuf[abuf]], writes=[r_psB[par]])
            if cm == 0 and ic < 3:
                emit_atb(ic + 1)
            p.op("v", lambda e, par=par, psB=psB: e.tensor_tensor(
                x2tile[par][:], psB, xtile[par][:], ALU.add),
                reads=[r_psB[par], r_xt[par]], writes=[r_x2t[par]])
            p.dma("y", f"x2t{par}", lambda e, ic=ic, cm=cm, par=par: e.dma_start(
                out=yout[128 * cm:128 * (cm + 1), 512 * ic:512 * (ic + 1)],
                in_=x2tile[par][:]),
                reads=[r_x2t[par]], writes=[r_out[cm]])

    p.barrier()

    r_gbrep, r_cbmul, r_sbrep = Res(), Res(), Res()
    r_x2b = [Res(), Res(), Res()]
    r_ln2s = Res()
    r_n2t1, r_n2bf = Res(), Res()
    r_h2T = Res()
    r_hs2 = [Res(), Res()]
    r_wd = Res()
    res_psF = [Res(), Res()]
    r_wg = [Res() for _ in range(4)]
    r_wu = [Res() for _ in range(4)]
    r_hf = [Res() for _ in range(4)]
    r_Uf = Res()
    r_af = [Res() for _ in range(4)]
    r_b = [Res() for _ in range(6)]
    r_k = [Res() for _ in range(4)]
    r_msc = [Res(), Res()]
    r_yt = [Res(), Res()]

    for ch in range(2):
        p.barrier()
        p.dma("g", "bcf2", lambda e: e.dma_start(out=cbmul[:], in_=modscr[4:5, :].broadcast_to([128, D])),
              reads=[r_modscr], writes=[r_cbmul])
        p.dma("g", "bcf3", lambda e: e.dma_start(out=sbrep[:], in_=modscr[3:4, :].broadcast_to([128, D])),
              reads=[r_modscr], writes=[r_sbrep])
        for t8 in range(8):
            par = t8 % 3
            rs = r_stats[par]
            row0 = CHUNK * ch + 128 * t8
            p.dma("y", f"x2b{par}", lambda e, row0=row0, par=par: e.dma_start(
                out=x2buf[par][:], in_=yout[row0:row0 + 128, :]),
                reads=[r_out[(CHUNK * ch + 128 * t8) // 128]],
                writes=[r_x2b[par]])
            for q4 in range(4):
                p.op("v", lambda e, par=par, q4=q4: e.bn_stats(
                    scols(par, 6 * q4, 6), x2buf[par][:, 512 * q4:512 * (q4 + 1)]),
                    reads=[r_x2b[par]], writes=[rs[0]])
            p.op("v", lambda e, par=par: e.bn_aggr(scols(par, 24, 2), scols(par, 0, 24)),
                 reads=[rs[0]], writes=[rs[1]])
            p.op("v", lambda e, par=par: e.tensor_scalar(
                scol(par, 26), scol(par, 25), 1.0, EPS, ALU.mult, ALU.add),
                reads=[rs[1]], writes=[rs[2]])
            p.op("s", lambda e, par=par: e.activation(scol(par, 27), scol(par, 26), AF.Ln),
                 reads=[rs[2]], writes=[rs[3]])
            p.op("s", lambda e, par=par: e.activation(scol(par, 28), scol(par, 27), AF.Exp, scale=-0.5),
                 reads=[rs[3]], writes=[rs[4]])
            p.op("v", lambda e, par=par: e.scalar_tensor_tensor(
                scol(par, 29), scol(par, 24), -1.0, scol(par, 28), ALU.mult, ALU.mult),
                reads=[rs[1], rs[4]], writes=[rs[5]])
            p.op("s", lambda e, par=par: e.activation(
                n2t1[:], x2buf[par][:], AF.Identity, bias=scol(par, 29), scale=scol(par, 28)),
                reads=[r_x2b[par], rs[4], rs[5]], writes=[r_n2t1])
            p.op("v", lambda e: e.tensor_tensor(ln2scr[:], n2t1[:], cbmul[:], ALU.mult),
                 reads=[r_n2t1, r_cbmul], writes=[r_ln2s])
            p.op("v", lambda e: e.tensor_tensor(n2bf[:], ln2scr[:], sbrep[:], ALU.add),
                 reads=[r_ln2s, r_sbrep], writes=[r_n2bf])
            for g in range(4):
                tpi = g % 2

                def mm_n2(e, g=g, tpi=tpi):
                    for i in range(4):
                        dk = 4 * g + i
                        e.transpose(tp[tpi][:, i, :],
                                    n2bf[:, 128 * dk:128 * (dk + 1)],
                                    id_sb[:])
                    return fence_tp(e, tpi)
                p.op("t", mm_n2, reads=[r_n2bf, r_id], writes=[r_tp[tpi]])
                p.op("v", lambda e, g=g, tpi=tpi, t8=t8: e.tensor_copy(
                    h2T[:, 4 * g:4 * g + 4, 128 * t8:128 * t8 + 128],
                    tp[tpi][:, :, :]),
                    reads=[r_tp[tpi]], writes=[r_h2T])

        p.barrier()

        pairs = [(0, 1), (0, 2), (0, 3), (1, 2), (1, 3), (2, 3)]
        cmp_after = {1: [0], 2: [1, 3], 3: [2, 4, 5]}

        def load_wgwu(cg, tau):
            j, k = tau // 4, tau % 4
            wpar = tau % 4
            p.dma("g", f"wg{wpar}", lambda e, cg=cg, j=j, k=k, wpar=wpar: e.dma_start(
                out=wg_sb[wpar][:, :, :], in_=wgP[cg, j, k]),
                writes=[r_wg[wpar]])
            p.dma("g", f"wu{wpar}", lambda e, cg=cg, j=j, k=k, wpar=wpar: e.dma_start(
                out=wu_sb[wpar][:, :, :], in_=wuP[cg, j, k]),
                writes=[r_wu[wpar]])

        def emit_wd(cg):
            p.dma("g", "wd", lambda e, cg=cg: e.dma_start(
                out=wd_sb[:, :, :],
                in_=wdP[cg].rearrange("j k p c -> p (j k) c")),
                writes=[r_wd])

        def ph1(cg):
            hsl, r_hsl = hs2[cg % 2], r_hs2[cg % 2]
            for j in range(2):
                for k in range(4):
                    tau = 4 * j + k
                    wpar = tau % 4
                    if tau >= 4:
                        load_wgwu(cg, tau)

                    def mm_G(e, wpar=wpar):
                        for dk in range(16):
                            for n in range(2):
                                ins = e.matmul(ps[1][:, 512 * n:512 * (n + 1)],
                                               wg_sb[wpar][:, dk, :],
                                               h2T[:, dk, 512 * n:512 * (n + 1)],
                                               start=(dk == 0), stop=(dk == 15))
                        return ins
                    p.op("t", mm_G, reads=[r_wg[wpar], r_h2T], writes=[res_psF[1]])

                    def mm_U(e, wpar=wpar):
                        for dk in range(16):
                            for n in range(2):
                                ins = e.matmul(ps[1][:, CHUNK + 512 * n:CHUNK + 512 * (n + 1)],
                                               wu_sb[wpar][:, dk, :],
                                               h2T[:, dk, 512 * n:512 * (n + 1)],
                                               start=(dk == 0), stop=(dk == 15))
                        return ins
                    p.op("t", mm_U, reads=[r_wu[wpar], r_h2T], writes=[res_psF[1]])
                    p.op("s", lambda e, k=k: e.activation(
                        hf[k][:], ps[1][:, 0:CHUNK], AF.Silu),
                        reads=[res_psF[1]], writes=[r_hf[k]])
                    p.op("v", lambda e: e.tensor_copy(Uf[:], ps[1][:, CHUNK:2 * CHUNK]),
                         reads=[res_psF[1]], writes=[r_Uf])
                    p.op("v", lambda e, k=k: e.tensor_tensor(af[k][:], hf[k][:], Uf[:], ALU.mult),
                         reads=[r_hf[k], r_Uf], writes=[r_af[k]])
                    p.op("s", lambda e, k=k: e.activation(hf[k][:], af[k][:], AF.Abs),
                         reads=[r_af[k]], writes=[r_hf[k]])
                    for bi_ in cmp_after.get(k, []):
                        a_, b_ = pairs[bi_]
                        p.op("v", lambda e, bi_=bi_, a_=a_, b_=b_: e.tensor_tensor(
                            bcmp[bi_][:], hf[a_][:], hf[b_][:], ALU.is_ge),
                            reads=[r_hf[a_], r_hf[b_]], writes=[r_b[bi_]])
                p.op("v", lambda e: e.tensor_tensor(msc[0][:], bcmp[0][:], bcmp[1][:], ALU.add),
                     reads=[r_b[0], r_b[1]], writes=[r_msc[0]])
                p.op("v", lambda e: e.tensor_tensor(msc[1][:], msc[0][:], bcmp[2][:], ALU.add),
                     reads=[r_msc[0], r_b[2]], writes=[r_msc[1]])
                p.op("v", lambda e: e.tensor_scalar(keep[0][:], msc[1][:], 1.5, None, ALU.is_ge),
                     reads=[r_msc[1]], writes=[r_k[0]])
                p.op("v", lambda e: e.tensor_tensor(msc[0][:], bcmp[3][:], bcmp[4][:], ALU.add),
                     reads=[r_b[3], r_b[4]], writes=[r_msc[0]])
                p.op("v", lambda e: e.tensor_tensor(msc[1][:], msc[0][:], bcmp[0][:], ALU.subtract),
                     reads=[r_msc[0], r_b[0]], writes=[r_msc[1]])
                p.op("v", lambda e: e.tensor_scalar(keep[1][:], msc[1][:], 0.5, None, ALU.is_ge),
                     reads=[r_msc[1]], writes=[r_k[1]])
                p.op("v", lambda e: e.tensor_tensor(msc[0][:], bcmp[1][:], bcmp[3][:], ALU.add),
                     reads=[r_b[1], r_b[3]], writes=[r_msc[0]])
                p.op("v", lambda e: e.tensor_tensor(msc[1][:], bcmp[5][:], msc[0][:], ALU.subtract),
                     reads=[r_msc[0], r_b[5]], writes=[r_msc[1]])
                p.op("v", lambda e: e.tensor_scalar(keep[2][:], msc[1][:], -0.5, None, ALU.is_ge),
                     reads=[r_msc[1]], writes=[r_k[2]])
                p.op("v", lambda e: e.tensor_tensor(msc[0][:], keep[0][:], keep[1][:], ALU.add),
                     reads=[r_k[0], r_k[1]], writes=[r_msc[0]])
                p.op("v", lambda e: e.tensor_tensor(msc[1][:], msc[0][:], keep[2][:], ALU.add),
                     reads=[r_msc[0], r_k[2]], writes=[r_msc[1]])
                p.op("v", lambda e: e.tensor_scalar(keep[3][:], msc[1][:], -1.0, 2.0, ALU.mult, ALU.add),
                     reads=[r_msc[1]], writes=[r_k[3]])
                for k in range(4):
                    tau = 4 * j + k
                    p.op("v", lambda e, k=k, tau=tau, hsl=hsl: e.tensor_tensor(
                        hsl[:, tau, :], af[k][:], keep[k][:], ALU.mult),
                        reads=[r_af[k], r_k[k]], writes=[r_hsl])

        def ph2(cg):
            hsl, r_hsl = hs2[cg % 2], r_hs2[cg % 2]
            for mi in range(8):
                ypar = mi % 2
                psY = ps[ypar]

                def mm_y(e, mi=mi, psY=psY, hsl=hsl):
                    for tau in range(8):
                        for n in range(4):
                            ins = e.matmul(psY[:, 512 * n:512 * (n + 1)],
                                           hsl[:, tau, 128 * mi:128 * (mi + 1)],
                                           wd_sb[:, tau, 512 * n:512 * (n + 1)],
                                           start=(tau == 0), stop=(tau == 7))
                    return ins
                p.op("t", mm_y, reads=[r_hsl, r_wd], writes=[res_psF[ypar]])
                p.op("v", lambda e, ypar=ypar, psY=psY: e.tensor_tensor(
                    ytile[ypar][:], psY[:, :], gbrep[:], ALU.mult),
                    reads=[res_psF[ypar], r_gbrep], writes=[r_yt[ypar]])
                row0 = CHUNK * ch + 128 * mi
                if not skip_ffn_accum:
                    p.dma("g", f"ac{mi}", lambda e, row0=row0, ypar=ypar: e.dma_start(
                        out=yout[row0:row0 + 128, :], in_=ytile[ypar][:],
                        accum_op=ALU.add),
                        reads=[r_yt[ypar]], writes=[r_out[row0 // 128]])

        for t_ in range(4):
            load_wgwu(0, t_)
        emit_wd(0)
        p.dma("g", "bcf", lambda e: e.dma_start(out=gbrep[:], in_=modscr[5:6, :].broadcast_to([128, D])),
              reads=[r_modscr], writes=[r_gbrep])
        ph1(0)
        for t_ in range(4):
            load_wgwu(1, t_)
        if dbg and ch == 0:
            for tau_ in range(8):
                p.dma("g", "dbg", lambda e, tau_=tau_: e.dma_start(
                    out=dbg_hs[tau_], in_=hs2[0][:, tau_, :]), reads=[r_hs2[0]])
        for cg in range(8):
            if cg < 7:
                ph1(cg + 1)
            if cg < 6:
                for t_ in range(4):
                    load_wgwu(cg + 2, t_)
            ph2(cg)
            if cg < 7:
                emit_wd(cg + 1)

    p.barrier()

    sems = {}
    from contextlib import ExitStack
    with ExitStack() as ctx:
        for name in list(p.cnt):
            sems[name] = ctx.enter_context(nc.semaphore(name))
        nc.all_engine_barrier()
        for name in list(p.cnt):
            nc.gpsimd.sem_clear(sems[name])
        nc.all_engine_barrier()
        p.replay(nc, sems)
    return nc


_CACHED = None


def _get_program():
    global _CACHED
    if _CACHED is None:
        _CACHED = build_program()
    return _CACHED


def kernel(L0_attn_x, L0_ffn_w_gate, L0_ffn_w_up, L0_ffn_w_down, L0_c, L0_wmod):
    x = np.ascontiguousarray(np.asarray(L0_attn_x, dtype=np.float32))
    wg = np.ascontiguousarray(np.asarray(L0_ffn_w_gate, dtype=np.float32))
    wu = np.ascontiguousarray(np.asarray(L0_ffn_w_up, dtype=np.float32))
    wd = np.ascontiguousarray(np.asarray(L0_ffn_w_down, dtype=np.float32))
    c = np.ascontiguousarray(np.asarray(L0_c, dtype=np.float32))
    wm = np.ascontiguousarray(np.asarray(L0_wmod, dtype=np.float32))

    wgP = np.ascontiguousarray(
        wg.reshape(16, 128, 8, 2, 128, 4).transpose(2, 3, 5, 1, 0, 4))
    wuP = np.ascontiguousarray(
        wu.reshape(16, 128, 8, 2, 128, 4).transpose(2, 3, 5, 1, 0, 4))
    wdP = np.ascontiguousarray(
        wd.reshape(8, 2, 128, 4, D).transpose(0, 1, 3, 2, 4))
    ident = np.eye(128, dtype=ml_dtypes.bfloat16)

    nc = _get_program()
    in_maps = []
    for core in range(NCORES):
        b, pp = core // 2, core % 2
        own = x[b, TOK * pp:TOK * (pp + 1), :]
        oth = x[b, TOK * (1 - pp):TOK * (2 - pp), :]
        xarr = np.ascontiguousarray(np.concatenate([own, oth], axis=0))
        in_maps.append(dict(xarr=xarr, crow=c[b:b + 1, :], wmod=wm,
                            wgP=wgP, wuP=wuP, wdP=wdP, ident=ident))
    res = run_bass_kernel_spmd(nc, in_maps, core_ids=list(range(NCORES)))
    out = np.empty((B, S, D), dtype=np.float32)
    for core in range(NCORES):
        b, pp = core // 2, core % 2
        out[b, TOK * pp:TOK * (pp + 1), :] = res.results[core]["yout"]
    return out


# version 18
# speedup vs baseline: 1.0086x; 1.0086x over previous
import numpy as np
import ml_dtypes
import concourse.bass as bass
import concourse.mybir as mybir
from concourse.bass_utils import run_bass_kernel_spmd

F32 = mybir.dt.float32
BF16 = mybir.dt.bfloat16
AF = mybir.ActivationFunctionType
ALU = mybir.AluOpType
AX = mybir.AxisListType

B, S, D = 4, 4096, 2048
FI = 8192
NCORES = 8
TOK = 2048
NT_ALL = 32
NT_OWN = 16
CHUNK = 1024
EPS = 1e-5
KB = 1024


class Tk:
    __slots__ = ("sem", "val", "stream", "compute")

    def __init__(self, sem, val, stream, compute):
        self.sem, self.val, self.stream, self.compute = sem, val, stream, compute


class Res:
    __slots__ = ("w", "rs")

    def __init__(self):
        self.w = None
        self.rs = []


class Plan:
    STREAMS = ("t", "v", "s", "g", "y")
    CSEM = {"t": "t", "v": "v", "s": "s", "g": "gc"}

    def __init__(self):
        self.ops = {st: [] for st in self.STREAMS}
        self.cnt = {"t": 0, "v": 0, "s": 0, "gc": 0}
        self.dma_slots = []

    def _do(self, stream, sem, amt, emit, reads, writes, waits):
        need = list(waits)
        for r in reads:
            if r.w is not None:
                need.append(r.w)
        for w in writes:
            if w.w is not None:
                need.append(w.w)
            need.extend(w.rs)
        if sem not in self.cnt:
            self.cnt[sem] = 0
            self.dma_slots.append(sem)
        self.cnt[sem] += amt
        tok = Tk(sem, self.cnt[sem], stream, amt == 1)
        self.ops[stream].append((need, emit, sem, amt))
        for r in reads:
            r.rs.append(tok)
        for w in writes:
            w.w = tok
            w.rs = []
        return tok

    def op(self, stream, emit, reads=(), writes=(), waits=()):
        return self._do(stream, self.CSEM[stream], 1, emit, reads, writes, waits)

    def dma(self, stream, slot, emit, reads=(), writes=(), waits=()):
        assert stream in ("g", "y")
        return self._do(stream, "q_" + slot, 16, emit, reads, writes, waits)

    def barrier(self):
        toks = []
        for sem, val in self.cnt.items():
            if sem in ("t", "v", "s", "gc"):
                st = {"t": "t", "v": "v", "s": "s", "gc": "g"}[sem]
                toks.append(Tk(sem, val, st, True))
            else:
                toks.append(Tk(sem, val, "?", False))
        for st in self.STREAMS:
            need = [t for t in toks if not (t.stream == st and t.compute)]
            self.ops[st].append((need, None, None, 0))

    def replay(self, nc, semobjs):
        engines = {"t": nc.tensor, "v": nc.vector, "s": nc.scalar,
                   "g": nc.gpsimd, "y": nc.sync}
        block_ctx = nc.Block()
        with block_ctx as block:
            def mk(st):
                def run(eng):
                    hwm = {}
                    for need, emit, sem, amt in self.ops[st]:
                        for w in need:
                            if st == "t" and w.stream == "t" and w.compute:
                                continue
                            if hwm.get(w.sem, 0) >= w.val:
                                continue
                            eng.wait_ge(semobjs[w.sem], w.val)
                            hwm[w.sem] = w.val
                        if emit is not None:
                            ins = emit(eng)
                            ins.then_inc(semobjs[sem], amt)
                return run
            block.tensor(mk("t"))
            block.vector(mk("v"))
            block.scalar(mk("s"))
            block.gpsimd(mk("g"))
            block.sync(mk("y"))


def build_program(dbg=False, skip_ffn_accum=False):
    nc = bass.Bass("TRN2", target_bir_lowering=False, debug=False,
                   detect_race_conditions=False)

    xarr = nc.dram_tensor("xarr", [S, D], F32, kind="ExternalInput").ap()
    crow = nc.dram_tensor("crow", [1, D], F32, kind="ExternalInput").ap()
    wmod = nc.dram_tensor("wmod", [D, 6 * D], F32, kind="ExternalInput").ap()
    wgP = nc.dram_tensor("wgP", [8, 2, 4, 128, 16, 128], F32, kind="ExternalInput").ap()
    wuP = nc.dram_tensor("wuP", [8, 2, 4, 128, 16, 128], F32, kind="ExternalInput").ap()
    wdP = nc.dram_tensor("wdP", [8, 2, 4, 128, D], F32, kind="ExternalInput").ap()
    ident = nc.dram_tensor("ident", [128, 128], BF16, kind="ExternalInput").ap()
    yout = nc.dram_tensor("yout", [TOK, D], F32, kind="ExternalOutput").ap()

    if dbg:
        dbg_mod = nc.dram_tensor("dbg_mod", [6, D], F32, kind="ExternalOutput").ap()
        dbg_H = nc.dram_tensor("dbg_H", [2, 128, D], F32, kind="ExternalOutput").ap()
        dbg_A = nc.dram_tensor("dbg_A", [2, 128, D], F32, kind="ExternalOutput").ap()
        dbg_hs = nc.dram_tensor("dbg_hs", [8, 128, CHUNK], F32, kind="ExternalOutput").ap()
        dbg2 = nc.dram_tensor("dbg2", [4, 128, D], F32, kind="ExternalOutput").ap()

    scrow = nc.dram_tensor("scrow", [1, D], BF16, kind="Internal").ap()
    modscr = nc.dram_tensor("modscr", [6, D], F32, kind="Internal").ap()
    atscr = nc.dram_tensor("atscr", [128, 16, 16, 128], BF16, kind="Internal").ap()

    BASE = (nc.sbuf_base + 511) // 512 * 512
    TOPB = nc.sbuf_top
    _sz = {F32: 4, BF16: 2}
    cur = [BASE]

    def A(name, shape, dt, at=None):
        nbytes = int(np.prod(shape[1:])) * _sz[dt]
        if at is None:
            at = cur[0]
            cur[0] = at + (nbytes + 255) // 256 * 256
        assert at + nbytes <= TOPB, (name, hex(at), nbytes, TOPB)
        return nc.alloc_sbuf_tensor_at(name, shape, dt, offset=at), at

    id_sb, _ = A("id_sb", [128, 128], BF16)
    sc_col, _ = A("sc_col", [128, 16], BF16)
    ga_col, _ = A("ga_col", [128, 16], F32)
    stats, _ = A("stats", [128, 128], F32)
    PHASE0 = cur[0]

    Hon, _ = A("Hon", [128, 16, D], BF16)
    Hot, o_Hot = A("Hot", [128, 16, D], BF16)
    camul, o_camul = A("camul", [128, D], F32)
    sarep, _ = A("sarep", [128, D], F32)
    xbuf = [A(f"xbuf{i}", [128, D], F32)[0] for i in range(3)]
    t1f, o_t1f = A("t1f", [128, D], F32)
    wmodbuf0, o_wmb = A("wmodbuf0", [128, D // 2], BF16)
    wmodbuf1, _ = A("wmodbuf1", [128, D // 2], BF16)
    wmodbufW = None
    A_sb, o_Asb = A("A_sb", [128, D], BF16)
    o_A2 = A("A2pad", [128, D], BF16)[1]
    wmodbuf = [A("wmbW0", [128, D], BF16, at=o_wmb)[0],
               A("wmbW1", [128, D], BF16, at=o_A2)[0]]
    modrow, o_modrow = A("modrow", [1, D], F32)
    modrow2, _ = A("modrow2", [1, D], F32)
    crow_sb, _ = A("crow_sb", [1, D], F32, at=o_Asb)
    scsil, _ = A("scsil", [1, D], BF16, at=o_modrow)
    ATch2 = [A(f"ATch{i}", [128, 16, 128], BF16, at=o_camul + i * 4096)[0]
             for i in range(2)]
    HT, _ = A("HT", [128, 16, D], BF16, at=o_Hot)
    ATbuf = [A(f"ATbuf{i}", [128, 16, 512], BF16, at=o_camul + i * 16384)[0]
             for i in range(2)]
    xtile = [A(f"xtile{i}", [128, 512], F32, at=o_t1f + i * 2048)[0]
             for i in range(2)]
    x2tile = [A(f"x2tile{i}", [128, 512], F32, at=o_t1f + 4096 + i * 2048)[0]
              for i in range(2)]

    cur[0] = PHASE0
    h2T, _ = A("h2T", [128, 16, CHUNK], BF16)
    hs2 = [A(f"hs{i}", [128, 8, CHUNK], BF16)[0] for i in range(2)]
    wd_sb, _ = A("wd_sb", [128, 8, D], BF16)
    wg_sb = [A(f"wg_sb{i}", [128, 16, 128], BF16)[0] for i in range(4)]
    wu_sb = [A(f"wu_sb{i}", [128, 16, 128], BF16)[0] for i in range(4)]
    hf = []
    o_hf0 = cur[0]
    for i in range(4):
        hf.append(A(f"hf{i}", [128, CHUNK], F32)[0])
    Uf, _ = A("Uf", [128, CHUNK], F32)
    o_ln2 = cur[0]
    af = [A(f"af{i}", [128, CHUNK], F32)[0] for i in range(4)]
    bcmp = [A(f"bcmp{i}", [128, CHUNK], BF16)[0] for i in range(6)]
    keep = [A(f"keep{i}", [128, CHUNK], BF16)[0] for i in range(4)]
    msc = [A(f"msc{i}", [128, CHUNK], BF16)[0] for i in range(2)]
    yt0, o_yt0 = A("ytile0", [128, D], BF16)
    yt1, o_yt1 = A("ytile1", [128, D], BF16)
    ytile = [yt0, yt1]
    gbrep, o_gbrep = A("gbrep", [128, D], F32)
    ln2scr, _ = A("ln2scr", [128, D], F32, at=o_gbrep)
    x2buf = [A(f"x2buf{i}", [128, D], F32, at=o_ln2 + i * 8192)[0] for i in range(3)]
    n2t1, _ = A("n2t1", [128, D], F32, at=o_ln2 + 24576)
    cbmul, _ = A("cbmul", [128, D], F32, at=o_ln2 + 32768)
    sbrep, _ = A("sbrep", [128, D], F32, at=o_ln2 + 40960)
    n2bf, _ = A("n2bf", [128, D], BF16, at=o_hf0)

    ps = [nc.alloc_psum_tensor(f"ps{i}", [128, D], F32) for i in range(2)]
    tp = [ps[1][:, 0:256].bitcast(BF16).rearrange("p (a b) -> p a b", a=4),
          ps[1][:, 512:768].bitcast(BF16).rearrange("p (a b) -> p a b", a=4)]
    tpdum = [ps[1][:, 256:258], ps[1][:, 768:770]]

    def fence_tp(e, tpi):
        return e.matmul(tpdum[tpi], id_sb[0:1, 0:128], id_sb[0:1, 0:2],
                        start=True, stop=True)

    p = Plan()

    def scol(setidx, q):
        c = setidx * 30 + q
        return stats[:, c:c + 1]

    def scols(setidx, q, w):
        c = setidx * 30 + q
        return stats[:, c:c + w]

    r_stats = [[Res() for _ in range(8)] for _ in range(4)]
    r_x = [Res() for _ in range(4)]
    r_H = Res()
    r_mod = Res()
    r_mod2 = Res()
    r_modscr = Res()
    r_scrow = Res()
    r_sccol = Res()
    r_gacol = Res()
    r_camul, r_sarep, r_t1 = Res(), Res(), Res()
    r_wmod = [Res(), Res()]
    r_crow, r_scsil = Res(), Res()
    r_id = Res()
    r_ps0, r_ps1 = Res(), Res()
    r_tp = [Res(), Res()]
    r_Asb = Res()
    r_ATch2 = [Res(), Res()]
    r_atscr = Res()
    r_zc = [Res() for _ in range(4)]

    zZ, zZi, zgaZ, znegm = (stats[:, 120 + i:121 + i] for i in range(4))

    p.dma("y", "ident", lambda e: e.dma_start(out=id_sb[:], in_=ident[:, :]),
          writes=[r_id])
    p.dma("y", "crow", lambda e: e.dma_start(out=crow_sb[:], in_=crow[:, :]),
          writes=[r_crow])
    p.op("s", lambda e: e.activation(scsil[:], crow_sb[:], AF.Silu),
         reads=[r_crow], writes=[r_scsil])
    p.dma("g", "scrow", lambda e: e.dma_start(out=scrow[:, :], in_=scsil[:, :]),
          reads=[r_scsil], writes=[r_scrow])
    for k in range(16):
        p.dma("g", "sccol", lambda e, k=k: e.dma_start(
            out=sc_col[:, k:k + 1],
            in_=scrow[0, 128 * k:128 * (k + 1)].rearrange("(p o) -> p o", o=1)),
            reads=[r_scrow], writes=[r_sccol])

    r_modps = Res()
    modps = [ps[1][0:1, 1024:1536], ps[1][0:1, 1536:2048]]
    sgnrow = t1f[0:1, :]

    def mod_mms_wide(sc, pst, wtoks):
        for k in range(16):
            par = k % 2
            p.dma("g", f"wm{par}", lambda e, k=k, par=par, sc=sc: e.dma_start(
                out=wmodbuf[par][:],
                in_=wmod[128 * k:128 * (k + 1), D * sc:D * (sc + 1)]),
                writes=[r_wmod[par]])

            def mm_mod(e, k=k, par=par, pst=pst):
                for n in range(4):
                    ins = e.matmul(pst[0:1, 512 * n:512 * (n + 1)],
                                   sc_col[:, k:k + 1],
                                   wmodbuf[par][:, 512 * n:512 * (n + 1)],
                                   start=(k == 0), stop=(k == 15))
                return ins
            p.op("t", mm_mod, reads=[r_sccol, r_wmod[par]], writes=wtoks)

    def mod_copy_wide(pst, rtoks):
        p.op("s", lambda e, pst=pst: e.activation(modrow[:], pst[0:1, :], AF.Identity),
             reads=rtoks, writes=[r_mod])

    def mod_mms_narrow(sc):
        for h in range(2):
            for k in range(16):
                par = k % 2
                c0 = D * sc + 1024 * h
                p.dma("g", f"wm{par}", lambda e, k=k, par=par, c0=c0: e.dma_start(
                    out=wmodbuf[par][:, 0:1024],
                    in_=wmod[128 * k:128 * (k + 1), c0:c0 + 1024]),
                    writes=[r_wmod[par]])

                def mm_mod(e, k=k, par=par):
                    for n in range(2):
                        ins = e.matmul(modps[n],
                                       sc_col[:, k:k + 1],
                                       wmodbuf[par][:, 512 * n:512 * (n + 1)],
                                       start=(k == 0), stop=(k == 15))
                    return ins
                p.op("t", mm_mod, reads=[r_sccol, r_wmod[par]], writes=[r_modps])
            p.op("s", lambda e, h=h: e.activation(
                modrow[0:1, 1024 * h:1024 * (h + 1)], ps[1][0:1, 1024:2048],
                AF.Identity),
                reads=[r_modps], writes=[r_mod])

    def mod_post(sc):
        if sc in (1, 4):
            p.op("v", lambda e: e.tensor_scalar(modrow2[:], modrow[:], 1.0, None, ALU.add),
                 reads=[r_mod], writes=[r_mod2])
            src_, r_src = modrow2, r_mod2
        elif sc in (2, 5):
            p.op("v", lambda e: e.tensor_scalar(
                modrow2[:], modrow[:], 0.0, 2.0, ALU.is_ge, ALU.mult),
                reads=[r_mod], writes=[r_mod2])
            p.op("v", lambda e: e.tensor_scalar(
                sgnrow, modrow2[:], -1.0, None, ALU.add),
                reads=[r_mod2], writes=[r_t1])
            p.op("v", lambda e: e.scalar_tensor_tensor(
                modrow2[:], modrow[:], -1.0, modrow[:], ALU.mult, ALU.max),
                reads=[r_mod], writes=[r_mod2])
            p.op("v", lambda e: e.scalar_tensor_tensor(
                modrow[:], modrow2[:], 0.25, sgnrow, ALU.max, ALU.mult),
                reads=[r_mod2, r_t1], writes=[r_mod])
            src_, r_src = modrow, r_mod
        else:
            src_, r_src = modrow, r_mod
        p.dma("g", "modw", lambda e, sc=sc, src_=src_: e.dma_start(
            out=modscr[sc:sc + 1, :], in_=src_[:, :]),
            reads=[r_src], writes=[r_modscr])

    def mod_chunk(sc):
        mod_mms_narrow(sc)
        mod_post(sc)

    mod_mms_wide(1, ps[0], [r_ps0])
    mod_mms_wide(0, ps[1], [r_modps, r_tp[0], r_tp[1]])

    def modulate(ct):
        Hsl = (Hon if ct < 16 else Hot)[:, ct % 16, :]
        p.op("v", lambda e, Hsl=Hsl: e.tensor_tensor(t1f[:], Hsl, camul[:], ALU.mult),
             reads=[r_H, r_camul], writes=[r_t1])
        p.op("v", lambda e, Hsl=Hsl: e.tensor_tensor(Hsl, t1f[:], sarep[:], ALU.add),
             reads=[r_t1, r_sarep], writes=[r_H])

    for ct in range(NT_ALL):
        par = ct % 3
        Hdst = (Hon if ct < 16 else Hot)[:, ct % 16, :]
        rs = r_stats[par]
        p.dma("y", f"xb{par}", lambda e, ct=ct, par=par: e.dma_start(
            out=xbuf[par][:], in_=xarr[128 * ct:128 * (ct + 1), :]),
            writes=[r_x[par]])
        for q4 in range(4):
            p.op("v", lambda e, par=par, q4=q4: e.bn_stats(
                scols(par, 6 * q4, 6), xbuf[par][:, 512 * q4:512 * (q4 + 1)]),
                reads=[r_x[par]], writes=[rs[0]])
        p.op("v", lambda e, par=par: e.bn_aggr(scols(par, 24, 2), scols(par, 0, 24)),
             reads=[rs[0]], writes=[rs[1]])
        p.op("v", lambda e, par=par: e.tensor_scalar(
            scol(par, 26), scol(par, 25), 1.0, EPS, ALU.mult, ALU.add),
            reads=[rs[1]], writes=[rs[2]])
        p.op("s", lambda e, par=par: e.activation(scol(par, 27), scol(par, 26), AF.Ln),
             reads=[rs[2]], writes=[rs[3]])
        p.op("s", lambda e, par=par: e.activation(scol(par, 28), scol(par, 27), AF.Exp, scale=-0.5),
             reads=[rs[3]], writes=[rs[4]])
        p.op("v", lambda e, par=par: e.scalar_tensor_tensor(
            scol(par, 29), scol(par, 24), -1.0, scol(par, 28), ALU.mult, ALU.mult),
            reads=[rs[1], rs[4]], writes=[rs[5]])
        p.op("s", lambda e, ct=ct, par=par, Hdst=Hdst: e.activation(
            Hdst, xbuf[par][:], AF.Identity, bias=scol(par, 29), scale=scol(par, 28)),
            reads=[r_x[par], rs[4], rs[5]], writes=[r_H])

    mod_copy_wide(ps[0], [r_ps0])
    mod_post(1)
    p.dma("g", "bca", lambda e: e.dma_start(out=camul[:], in_=modscr[1:2, :].broadcast_to([128, D])),
          reads=[r_modscr], writes=[r_camul])
    mod_copy_wide(ps[1], [r_modps, r_tp[0], r_tp[1]])
    mod_post(0)
    p.dma("g", "bcs", lambda e: e.dma_start(out=sarep[:], in_=modscr[0:1, :].broadcast_to([128, D])),
          reads=[r_modscr], writes=[r_sarep])
    for ct in range(NT_ALL):
        modulate(ct)

    mod_mms_wide(2, ps[0], [r_ps0])
    mod_copy_wide(ps[0], [r_ps0])
    mod_post(2)
    for k in range(16):
        p.dma("g", "gacol", lambda e, k=k: e.dma_start(
            out=ga_col[:, k:k + 1],
            in_=modscr[2, 128 * k:128 * (k + 1)].rearrange("(p o) -> p o", o=1)),
            reads=[r_modscr], writes=[r_gacol])

    if dbg:
        p.dma("g", "dbg", lambda e: e.dma_start(out=dbg_mod[:, :], in_=modscr[:, :]),
              reads=[r_modscr])
        p.dma("g", "dbg", lambda e: e.dma_start(out=dbg2[0], in_=Hon[:, 0, :]), reads=[r_H])
        p.dma("g", "dbg", lambda e: e.dma_start(out=dbg2[1], in_=camul[:]), reads=[r_camul])
        p.dma("g", "dbg", lambda e: e.dma_start(out=dbg2[2], in_=sarep[:]), reads=[r_sarep])
        p.dma("g", "dbg", lambda e: e.dma_start(out=dbg2[3][:, 0:16], in_=ga_col[:, :]),
              reads=[r_gacol])
        p.dma("g", "dbg", lambda e: e.dma_start(out=dbg2[3][:, 32:64], in_=stats[:, :]),
              reads=[r_stats[0][5], r_stats[1][5]])

    if dbg:
        p.dma("g", "dbg", lambda e: e.dma_start(out=dbg_H[0], in_=Hon[:, 0, :]), reads=[r_H])
        p.dma("g", "dbg", lambda e: e.dma_start(out=dbg_H[1], in_=Hot[:, 0, :]), reads=[r_H])

    def Hsl(k):
        return (Hon if k < 16 else Hot)[:, k % 16, :]

    pend_tp = None
    for bi in range(16):
        def mm_E(e, bi=bi):
            for k in range(NT_ALL):
                h = Hsl(k)
                for n in range(4):
                    ins = e.matmul(ps[0][:, 512 * n:512 * (n + 1)],
                                   h[:, 128 * bi:128 * (bi + 1)],
                                   h[:, 512 * n:512 * (n + 1)],
                                   start=(k == 0), stop=(k == NT_ALL - 1))
            return ins
        p.op("t", mm_E, reads=[r_H], writes=[r_ps0])
        if bi == 3:
            mod_chunk(3)
        elif bi == 6:
            mod_chunk(4)
        elif bi == 9:
            mod_chunk(5)

        if pend_tp is not None:
            pend_tp()
            pend_tp = None

        p.op("v", lambda e: e.reduce_max(znegm, ps[0][:], axis=AX.X, negate=True),
             reads=[r_ps0], writes=[r_zc[3]])
        p.op("s", lambda e: e.activation(A_sb[:], ps[0][:], AF.Exp,
                                         bias=znegm),
             reads=[r_ps0, r_zc[3]], writes=[r_Asb])
        p.op("v", lambda e: e.reduce_sum(zZ, A_sb[:], axis=AX.X),
             reads=[r_Asb], writes=[r_zc[0]])
        p.op("v", lambda e: e.reciprocal(zZi, zZ),
             reads=[r_zc[0]], writes=[r_zc[1]])
        p.op("v", lambda e, bi=bi: e.tensor_tensor(zgaZ, zZi, ga_col[:, bi:bi + 1], ALU.mult),
             reads=[r_zc[1], r_gacol], writes=[r_zc[2]])
        p.op("v", lambda e: e.tensor_scalar(A_sb[:], A_sb[:], zgaZ, None, ALU.mult),
             reads=[r_Asb, r_zc[2]], writes=[r_Asb])
        if dbg and bi == 0:
            p.dma("g", "dbg", lambda e: e.dma_start(out=dbg_A[1], in_=A_sb[:]), reads=[r_Asb])

        def make_tp(bi=bi):
            def emit_all():
                ach = ATch2[bi % 2]
                rach = r_ATch2[bi % 2]
                for g in range(4):
                    tpi = g % 2

                    def mm_tp(e, g=g, tpi=tpi, bi=bi):
                        for i in range(4):
                            e.transpose(tp[tpi][:, i, :],
                                        A_sb[:, 128 * (4 * g + i):128 * (4 * g + i + 1)],
                                        id_sb[:])
                        return fence_tp(e, tpi)
                    p.op("t", mm_tp, reads=[r_Asb, r_id], writes=[r_tp[tpi]])
                    p.op("v", lambda e, g=g, tpi=tpi, ach=ach, rach=rach: e.tensor_copy(
                        ach[:, 4 * g:4 * g + 4, :],
                        tp[tpi][:, :, :]),
                        reads=[r_tp[tpi]], writes=[rach, r_camul, r_sarep])
                p.dma("y", "atsp", lambda e, bi=bi, ach=ach: e.dma_start(
                    out=atscr[:, :, bi, :], in_=ach[:, :, :]),
                    reads=[rach], writes=[r_atscr])
            return emit_all
        pend_tp = make_tp()
        if bi == 15:
            pend_tp()
            pend_tp = None

    p.barrier()

    r_HT = Res()
    r_atbuf = [Res(), Res()]
    r_xt = [Res(), Res()]
    r_x2t = [Res(), Res()]
    r_psB = [Res(), Res()]
    r_out = [Res() for _ in range(16)]
    def emit_atb(ic):
        abuf = ic % 2
        p.dma("y", f"atb{abuf}", lambda e, ic=ic, abuf=abuf: e.dma_start(
            out=ATbuf[abuf][:, :, :],
            in_=atscr[:, :, 4 * ic:4 * (ic + 1), :].rearrange("p jk b c -> p jk (b c)")),
            reads=[r_atscr], writes=[r_atbuf[abuf]])

    emit_atb(0)
    for ic in range(4):
        abuf = ic % 2
        for cm in range(16):
            par = cm % 2
            psB = ps[0][:, 512 * par:512 * (par + 1)]
            if ic == 0:
                for g in range(4):
                    tpi = g % 2

                    def mm_ht(e, cm=cm, g=g, tpi=tpi):
                        for i in range(4):
                            jk = 4 * g + i
                            e.transpose(tp[tpi][:, i, :],
                                        Hon[:, cm, 128 * jk:128 * (jk + 1)],
                                        id_sb[:])
                        return fence_tp(e, tpi)
                    p.op("t", mm_ht, reads=[r_H, r_id], writes=[r_tp[tpi]])
                    p.op("v", lambda e, cm=cm, g=g, tpi=tpi: e.tensor_copy(
                        HT[:, 4 * g:4 * g + 4, 128 * cm:128 * cm + 128],
                        tp[tpi][:, :, :]),
                        reads=[r_tp[tpi]], writes=[r_HT])
            p.dma("y", f"xt{par}", lambda e, ic=ic, cm=cm, par=par: e.dma_start(
                out=xtile[par][:],
                in_=xarr[128 * cm:128 * (cm + 1), 512 * ic:512 * (ic + 1)]),
                writes=[r_xt[par]])

            def mm_out(e, ic=ic, cm=cm, par=par, abuf=abuf, psB=psB):
                for jk in range(16):
                    ins = e.matmul(psB, HT[:, jk, 128 * cm:128 * (cm + 1)],
                                   ATbuf[abuf][:, jk, :],
                                   start=(jk == 0), stop=(jk == 15))
                return ins
            wt = [r_psB[par]] + ([r_ps0] if ic == 0 and cm < 2 else [])
            p.op("t", mm_out, reads=[r_HT, r_atbuf[abuf]], writes=wt)
            if cm == 0 and ic < 3:
                emit_atb(ic + 1)
            p.op("v", lambda e, par=par, psB=psB: e.tensor_tensor(
                x2tile[par][:], psB, xtile[par][:], ALU.add),
                reads=[r_psB[par], r_xt[par]], writes=[r_x2t[par]])
            p.dma("y", f"x2t{par}", lambda e, ic=ic, cm=cm, par=par: e.dma_start(
                out=yout[128 * cm:128 * (cm + 1), 512 * ic:512 * (ic + 1)],
                in_=x2tile[par][:]),
                reads=[r_x2t[par]], writes=[r_out[cm]])

    p.barrier()

    r_gbrep, r_cbmul, r_sbrep = Res(), Res(), Res()
    r_x2b = [Res(), Res(), Res()]
    r_ln2s = Res()
    r_n2t1, r_n2bf = Res(), Res()
    r_h2T = Res()
    r_hs2 = [Res(), Res()]
    r_wd = Res()
    res_psF = [Res(), Res()]
    r_wg = [Res() for _ in range(4)]
    r_wu = [Res() for _ in range(4)]
    r_hf = [Res() for _ in range(4)]
    r_Uf = Res()
    r_af = [Res() for _ in range(4)]
    r_b = [Res() for _ in range(6)]
    r_k = [Res() for _ in range(4)]
    r_msc = [Res(), Res()]
    r_yt = [Res(), Res()]

    for ch in range(2):
        p.barrier()
        p.dma("g", "bcf2", lambda e: e.dma_start(out=cbmul[:], in_=modscr[4:5, :].broadcast_to([128, D])),
              reads=[r_modscr], writes=[r_cbmul])
        p.dma("g", "bcf3", lambda e: e.dma_start(out=sbrep[:], in_=modscr[3:4, :].broadcast_to([128, D])),
              reads=[r_modscr], writes=[r_sbrep])
        for t8 in range(8):
            par = t8 % 3
            rs = r_stats[par]
            row0 = CHUNK * ch + 128 * t8
            p.dma("y", f"x2b{par}", lambda e, row0=row0, par=par: e.dma_start(
                out=x2buf[par][:], in_=yout[row0:row0 + 128, :]),
                reads=[r_out[(CHUNK * ch + 128 * t8) // 128]],
                writes=[r_x2b[par]])
            for q4 in range(4):
                p.op("v", lambda e, par=par, q4=q4: e.bn_stats(
                    scols(par, 6 * q4, 6), x2buf[par][:, 512 * q4:512 * (q4 + 1)]),
                    reads=[r_x2b[par]], writes=[rs[0]])
            p.op("v", lambda e, par=par: e.bn_aggr(scols(par, 24, 2), scols(par, 0, 24)),
                 reads=[rs[0]], writes=[rs[1]])
            p.op("v", lambda e, par=par: e.tensor_scalar(
                scol(par, 26), scol(par, 25), 1.0, EPS, ALU.mult, ALU.add),
                reads=[rs[1]], writes=[rs[2]])
            p.op("s", lambda e, par=par: e.activation(scol(par, 27), scol(par, 26), AF.Ln),
                 reads=[rs[2]], writes=[rs[3]])
            p.op("s", lambda e, par=par: e.activation(scol(par, 28), scol(par, 27), AF.Exp, scale=-0.5),
                 reads=[rs[3]], writes=[rs[4]])
            p.op("v", lambda e, par=par: e.scalar_tensor_tensor(
                scol(par, 29), scol(par, 24), -1.0, scol(par, 28), ALU.mult, ALU.mult),
                reads=[rs[1], rs[4]], writes=[rs[5]])
            p.op("s", lambda e, par=par: e.activation(
                n2t1[:], x2buf[par][:], AF.Identity, bias=scol(par, 29), scale=scol(par, 28)),
                reads=[r_x2b[par], rs[4], rs[5]], writes=[r_n2t1])
            p.op("v", lambda e: e.tensor_tensor(ln2scr[:], n2t1[:], cbmul[:], ALU.mult),
                 reads=[r_n2t1, r_cbmul], writes=[r_ln2s])
            p.op("v", lambda e: e.tensor_tensor(n2bf[:], ln2scr[:], sbrep[:], ALU.add),
                 reads=[r_ln2s, r_sbrep], writes=[r_n2bf])
            for g in range(4):
                tpi = g % 2

                def mm_n2(e, g=g, tpi=tpi):
                    for i in range(4):
                        dk = 4 * g + i
                        e.transpose(tp[tpi][:, i, :],
                                    n2bf[:, 128 * dk:128 * (dk + 1)],
                                    id_sb[:])
                    return fence_tp(e, tpi)
                p.op("t", mm_n2, reads=[r_n2bf, r_id], writes=[r_tp[tpi]])
                p.op("v", lambda e, g=g, tpi=tpi, t8=t8: e.tensor_copy(
                    h2T[:, 4 * g:4 * g + 4, 128 * t8:128 * t8 + 128],
                    tp[tpi][:, :, :]),
                    reads=[r_tp[tpi]], writes=[r_h2T])

        p.barrier()

        pairs = [(0, 1), (0, 2), (0, 3), (1, 2), (1, 3), (2, 3)]
        cmp_after = {1: [0], 2: [1, 3], 3: [2, 4, 5]}

        def load_wgwu(cg, tau):
            j, k = tau // 4, tau % 4
            wpar = tau % 4
            p.dma("g", f"wg{wpar}", lambda e, cg=cg, j=j, k=k, wpar=wpar: e.dma_start(
                out=wg_sb[wpar][:, :, :], in_=wgP[cg, j, k]),
                writes=[r_wg[wpar]])
            p.dma("g", f"wu{wpar}", lambda e, cg=cg, j=j, k=k, wpar=wpar: e.dma_start(
                out=wu_sb[wpar][:, :, :], in_=wuP[cg, j, k]),
                writes=[r_wu[wpar]])

        def emit_wd(cg):
            p.dma("g", "wd", lambda e, cg=cg: e.dma_start(
                out=wd_sb[:, :, :],
                in_=wdP[cg].rearrange("j k p c -> p (j k) c")),
                writes=[r_wd])

        def ph1(cg):
            hsl, r_hsl = hs2[cg % 2], r_hs2[cg % 2]
            for j in range(2):
                for k in range(4):
                    tau = 4 * j + k
                    wpar = tau % 4
                    if tau >= 4:
                        load_wgwu(cg, tau)

                    def mm_G(e, wpar=wpar):
                        for dk in range(16):
                            for n in range(2):
                                ins = e.matmul(ps[1][:, 512 * n:512 * (n + 1)],
                                               wg_sb[wpar][:, dk, :],
                                               h2T[:, dk, 512 * n:512 * (n + 1)],
                                               start=(dk == 0), stop=(dk == 15))
                        return ins
                    p.op("t", mm_G, reads=[r_wg[wpar], r_h2T], writes=[res_psF[1]])

                    def mm_U(e, wpar=wpar):
                        for dk in range(16):
                            for n in range(2):
                                ins = e.matmul(ps[1][:, CHUNK + 512 * n:CHUNK + 512 * (n + 1)],
                                               wu_sb[wpar][:, dk, :],
                                               h2T[:, dk, 512 * n:512 * (n + 1)],
                                               start=(dk == 0), stop=(dk == 15))
                        return ins
                    p.op("t", mm_U, reads=[r_wu[wpar], r_h2T], writes=[res_psF[1]])
                    p.op("s", lambda e, k=k: e.activation(
                        hf[k][:], ps[1][:, 0:CHUNK], AF.Silu),
                        reads=[res_psF[1]], writes=[r_hf[k]])
                    p.op("v", lambda e: e.tensor_copy(Uf[:], ps[1][:, CHUNK:2 * CHUNK]),
                         reads=[res_psF[1]], writes=[r_Uf])
                    p.op("v", lambda e, k=k: e.tensor_tensor(af[k][:], hf[k][:], Uf[:], ALU.mult),
                         reads=[r_hf[k], r_Uf], writes=[r_af[k]])
                    p.op("s", lambda e, k=k: e.activation(hf[k][:], af[k][:], AF.Abs),
                         reads=[r_af[k]], writes=[r_hf[k]])
                    for bi_ in cmp_after.get(k, []):
                        a_, b_ = pairs[bi_]
                        p.op("v", lambda e, bi_=bi_, a_=a_, b_=b_: e.tensor_tensor(
                            bcmp[bi_][:], hf[a_][:], hf[b_][:], ALU.is_ge),
                            reads=[r_hf[a_], r_hf[b_]], writes=[r_b[bi_]])
                p.op("v", lambda e: e.tensor_tensor(msc[0][:], bcmp[0][:], bcmp[1][:], ALU.add),
                     reads=[r_b[0], r_b[1]], writes=[r_msc[0]])
                p.op("v", lambda e: e.tensor_tensor(msc[1][:], msc[0][:], bcmp[2][:], ALU.add),
                     reads=[r_msc[0], r_b[2]], writes=[r_msc[1]])
                p.op("v", lambda e: e.tensor_scalar(keep[0][:], msc[1][:], 1.5, None, ALU.is_ge),
                     reads=[r_msc[1]], writes=[r_k[0]])
                p.op("v", lambda e: e.tensor_tensor(msc[0][:], bcmp[3][:], bcmp[4][:], ALU.add),
                     reads=[r_b[3], r_b[4]], writes=[r_msc[0]])
                p.op("v", lambda e: e.tensor_tensor(msc[1][:], msc[0][:], bcmp[0][:], ALU.subtract),
                     reads=[r_msc[0], r_b[0]], writes=[r_msc[1]])
                p.op("v", lambda e: e.tensor_scalar(keep[1][:], msc[1][:], 0.5, None, ALU.is_ge),
                     reads=[r_msc[1]], writes=[r_k[1]])
                p.op("v", lambda e: e.tensor_tensor(msc[0][:], bcmp[1][:], bcmp[3][:], ALU.add),
                     reads=[r_b[1], r_b[3]], writes=[r_msc[0]])
                p.op("v", lambda e: e.tensor_tensor(msc[1][:], bcmp[5][:], msc[0][:], ALU.subtract),
                     reads=[r_msc[0], r_b[5]], writes=[r_msc[1]])
                p.op("v", lambda e: e.tensor_scalar(keep[2][:], msc[1][:], -0.5, None, ALU.is_ge),
                     reads=[r_msc[1]], writes=[r_k[2]])
                p.op("v", lambda e: e.tensor_tensor(msc[0][:], keep[0][:], keep[1][:], ALU.add),
                     reads=[r_k[0], r_k[1]], writes=[r_msc[0]])
                p.op("v", lambda e: e.tensor_tensor(msc[1][:], msc[0][:], keep[2][:], ALU.add),
                     reads=[r_msc[0], r_k[2]], writes=[r_msc[1]])
                p.op("v", lambda e: e.tensor_scalar(keep[3][:], msc[1][:], -1.0, 2.0, ALU.mult, ALU.add),
                     reads=[r_msc[1]], writes=[r_k[3]])
                for k in range(4):
                    tau = 4 * j + k
                    p.op("v", lambda e, k=k, tau=tau, hsl=hsl: e.tensor_tensor(
                        hsl[:, tau, :], af[k][:], keep[k][:], ALU.mult),
                        reads=[r_af[k], r_k[k]], writes=[r_hsl])

        def ph2(cg):
            hsl, r_hsl = hs2[cg % 2], r_hs2[cg % 2]
            for mi in range(8):
                ypar = mi % 2
                psY = ps[ypar]

                def mm_y(e, mi=mi, psY=psY, hsl=hsl):
                    for tau in range(8):
                        for n in range(4):
                            ins = e.matmul(psY[:, 512 * n:512 * (n + 1)],
                                           hsl[:, tau, 128 * mi:128 * (mi + 1)],
                                           wd_sb[:, tau, 512 * n:512 * (n + 1)],
                                           start=(tau == 0), stop=(tau == 7))
                    return ins
                p.op("t", mm_y, reads=[r_hsl, r_wd], writes=[res_psF[ypar]])
                p.op("v", lambda e, ypar=ypar, psY=psY: e.tensor_tensor(
                    ytile[ypar][:], psY[:, :], gbrep[:], ALU.mult),
                    reads=[res_psF[ypar], r_gbrep], writes=[r_yt[ypar]])
                row0 = CHUNK * ch + 128 * mi
                if not skip_ffn_accum:
                    p.dma("g", f"ac{mi}", lambda e, row0=row0, ypar=ypar: e.dma_start(
                        out=yout[row0:row0 + 128, :], in_=ytile[ypar][:],
                        accum_op=ALU.add),
                        reads=[r_yt[ypar]], writes=[r_out[row0 // 128]])

        for t_ in range(4):
            load_wgwu(0, t_)
        emit_wd(0)
        p.dma("g", "bcf", lambda e: e.dma_start(out=gbrep[:], in_=modscr[5:6, :].broadcast_to([128, D])),
              reads=[r_modscr], writes=[r_gbrep])
        ph1(0)
        for t_ in range(4):
            load_wgwu(1, t_)
        if dbg and ch == 0:
            for tau_ in range(8):
                p.dma("g", "dbg", lambda e, tau_=tau_: e.dma_start(
                    out=dbg_hs[tau_], in_=hs2[0][:, tau_, :]), reads=[r_hs2[0]])
        for cg in range(8):
            if cg < 7:
                ph1(cg + 1)
            if cg < 6:
                for t_ in range(4):
                    load_wgwu(cg + 2, t_)
            ph2(cg)
            if cg < 7:
                emit_wd(cg + 1)

    p.barrier()

    sems = {}
    from contextlib import ExitStack
    with ExitStack() as ctx:
        for name in list(p.cnt):
            sems[name] = ctx.enter_context(nc.semaphore(name))
        nc.all_engine_barrier()
        for name in list(p.cnt):
            nc.gpsimd.sem_clear(sems[name])
        nc.all_engine_barrier()
        p.replay(nc, sems)
    return nc


_CACHED = None


def _get_program():
    global _CACHED
    if _CACHED is None:
        _CACHED = build_program()
    return _CACHED


def kernel(L0_attn_x, L0_ffn_w_gate, L0_ffn_w_up, L0_ffn_w_down, L0_c, L0_wmod):
    x = np.ascontiguousarray(np.asarray(L0_attn_x, dtype=np.float32))
    wg = np.ascontiguousarray(np.asarray(L0_ffn_w_gate, dtype=np.float32))
    wu = np.ascontiguousarray(np.asarray(L0_ffn_w_up, dtype=np.float32))
    wd = np.ascontiguousarray(np.asarray(L0_ffn_w_down, dtype=np.float32))
    c = np.ascontiguousarray(np.asarray(L0_c, dtype=np.float32))
    wm = np.ascontiguousarray(np.asarray(L0_wmod, dtype=np.float32))

    wgP = np.ascontiguousarray(
        wg.reshape(16, 128, 8, 2, 128, 4).transpose(2, 3, 5, 1, 0, 4))
    wuP = np.ascontiguousarray(
        wu.reshape(16, 128, 8, 2, 128, 4).transpose(2, 3, 5, 1, 0, 4))
    wdP = np.ascontiguousarray(
        wd.reshape(8, 2, 128, 4, D).transpose(0, 1, 3, 2, 4))
    ident = np.eye(128, dtype=ml_dtypes.bfloat16)

    nc = _get_program()
    in_maps = []
    for core in range(NCORES):
        b, pp = core // 2, core % 2
        own = x[b, TOK * pp:TOK * (pp + 1), :]
        oth = x[b, TOK * (1 - pp):TOK * (2 - pp), :]
        xarr = np.ascontiguousarray(np.concatenate([own, oth], axis=0))
        in_maps.append(dict(xarr=xarr, crow=c[b:b + 1, :], wmod=wm,
                            wgP=wgP, wuP=wuP, wdP=wdP, ident=ident))
    res = run_bass_kernel_spmd(nc, in_maps, core_ids=list(range(NCORES)))
    out = np.empty((B, S, D), dtype=np.float32)
    for core in range(NCORES):
        b, pp = core // 2, core % 2
        out[b, TOK * pp:TOK * (pp + 1), :] = res.results[core]["yout"]
    return out


# version 19
# speedup vs baseline: 1.0139x; 1.0052x over previous
import numpy as np
import ml_dtypes
import concourse.bass as bass
import concourse.mybir as mybir
from concourse.bass_utils import run_bass_kernel_spmd

F32 = mybir.dt.float32
BF16 = mybir.dt.bfloat16
AF = mybir.ActivationFunctionType
ALU = mybir.AluOpType
AX = mybir.AxisListType

B, S, D = 4, 4096, 2048
FI = 8192
NCORES = 8
TOK = 2048
NT_ALL = 32
NT_OWN = 16
CHUNK = 1024
EPS = 1e-5
KB = 1024


class Tk:
    __slots__ = ("sem", "val", "stream", "compute")

    def __init__(self, sem, val, stream, compute):
        self.sem, self.val, self.stream, self.compute = sem, val, stream, compute


class Res:
    __slots__ = ("w", "rs")

    def __init__(self):
        self.w = None
        self.rs = []


class Plan:
    STREAMS = ("t", "v", "s", "g", "y")
    CSEM = {"t": "t", "v": "v", "s": "s", "g": "gc"}

    def __init__(self):
        self.ops = {st: [] for st in self.STREAMS}
        self.cnt = {"t": 0, "v": 0, "s": 0, "gc": 0}
        self.dma_slots = []

    def _do(self, stream, sem, amt, emit, reads, writes, waits):
        need = list(waits)
        for r in reads:
            if r.w is not None:
                need.append(r.w)
        for w in writes:
            if w.w is not None:
                need.append(w.w)
            need.extend(w.rs)
        if sem not in self.cnt:
            self.cnt[sem] = 0
            self.dma_slots.append(sem)
        self.cnt[sem] += amt
        tok = Tk(sem, self.cnt[sem], stream, amt == 1)
        self.ops[stream].append((need, emit, sem, amt))
        for r in reads:
            r.rs.append(tok)
        for w in writes:
            w.w = tok
            w.rs = []
        return tok

    def op(self, stream, emit, reads=(), writes=(), waits=()):
        return self._do(stream, self.CSEM[stream], 1, emit, reads, writes, waits)

    def dma(self, stream, slot, emit, reads=(), writes=(), waits=()):
        assert stream in ("g", "y")
        return self._do(stream, "q_" + slot, 16, emit, reads, writes, waits)

    def barrier(self):
        toks = []
        for sem, val in self.cnt.items():
            if sem in ("t", "v", "s", "gc"):
                st = {"t": "t", "v": "v", "s": "s", "gc": "g"}[sem]
                toks.append(Tk(sem, val, st, True))
            else:
                toks.append(Tk(sem, val, "?", False))
        for st in self.STREAMS:
            need = [t for t in toks if not (t.stream == st and t.compute)]
            self.ops[st].append((need, None, None, 0))

    def replay(self, nc, semobjs):
        engines = {"t": nc.tensor, "v": nc.vector, "s": nc.scalar,
                   "g": nc.gpsimd, "y": nc.sync}
        block_ctx = nc.Block()
        with block_ctx as block:
            def mk(st):
                def run(eng):
                    hwm = {}
                    for need, emit, sem, amt in self.ops[st]:
                        for w in need:
                            if st == "t" and w.stream == "t" and w.compute:
                                continue
                            if hwm.get(w.sem, 0) >= w.val:
                                continue
                            eng.wait_ge(semobjs[w.sem], w.val)
                            hwm[w.sem] = w.val
                        if emit is not None:
                            ins = emit(eng)
                            ins.then_inc(semobjs[sem], amt)
                return run
            block.tensor(mk("t"))
            block.vector(mk("v"))
            block.scalar(mk("s"))
            block.gpsimd(mk("g"))
            block.sync(mk("y"))


def build_program(dbg=False, skip_ffn_accum=False):
    nc = bass.Bass("TRN2", target_bir_lowering=False, debug=False,
                   detect_race_conditions=False)

    xarr = nc.dram_tensor("xarr", [S, D], F32, kind="ExternalInput").ap()
    crow = nc.dram_tensor("crow", [1, D], F32, kind="ExternalInput").ap()
    wmod = nc.dram_tensor("wmod", [D, 6 * D], F32, kind="ExternalInput").ap()
    wgP = nc.dram_tensor("wgP", [8, 2, 4, 128, 16, 128], F32, kind="ExternalInput").ap()
    wuP = nc.dram_tensor("wuP", [8, 2, 4, 128, 16, 128], F32, kind="ExternalInput").ap()
    wdP = nc.dram_tensor("wdP", [8, 2, 4, 128, D], F32, kind="ExternalInput").ap()
    ident = nc.dram_tensor("ident", [128, 128], BF16, kind="ExternalInput").ap()
    yout = nc.dram_tensor("yout", [TOK, D], F32, kind="ExternalOutput").ap()

    if dbg:
        dbg_mod = nc.dram_tensor("dbg_mod", [6, D], F32, kind="ExternalOutput").ap()
        dbg_H = nc.dram_tensor("dbg_H", [2, 128, D], F32, kind="ExternalOutput").ap()
        dbg_A = nc.dram_tensor("dbg_A", [2, 128, D], F32, kind="ExternalOutput").ap()
        dbg_hs = nc.dram_tensor("dbg_hs", [8, 128, CHUNK], F32, kind="ExternalOutput").ap()
        dbg2 = nc.dram_tensor("dbg2", [4, 128, D], F32, kind="ExternalOutput").ap()

    scrow = nc.dram_tensor("scrow", [1, D], BF16, kind="Internal").ap()
    modscr = nc.dram_tensor("modscr", [6, D], F32, kind="Internal").ap()
    atscr = nc.dram_tensor("atscr", [128, 16, 16, 128], BF16, kind="Internal").ap()

    BASE = (nc.sbuf_base + 511) // 512 * 512
    TOPB = nc.sbuf_top
    _sz = {F32: 4, BF16: 2}
    cur = [BASE]

    def A(name, shape, dt, at=None):
        nbytes = int(np.prod(shape[1:])) * _sz[dt]
        if at is None:
            at = cur[0]
            cur[0] = at + (nbytes + 255) // 256 * 256
        assert at + nbytes <= TOPB, (name, hex(at), nbytes, TOPB)
        return nc.alloc_sbuf_tensor_at(name, shape, dt, offset=at), at

    id_sb, _ = A("id_sb", [128, 128], BF16)
    sc_col, _ = A("sc_col", [128, 16], BF16)
    ga_col, _ = A("ga_col", [128, 16], F32)
    stats, _ = A("stats", [128, 128], F32)
    PHASE0 = cur[0]

    Hon, _ = A("Hon", [128, 16, D], BF16)
    Hot, o_Hot = A("Hot", [128, 16, D], BF16)
    camul, o_camul = A("camul", [128, D], F32)
    sarep, _ = A("sarep", [128, D], F32)
    xbuf = [A(f"xbuf{i}", [128, D], F32)[0] for i in range(3)]
    t1f, o_t1f = A("t1f", [128, D], F32)
    wmodbuf0, o_wmb = A("wmodbuf0", [128, D // 2], BF16)
    wmodbuf1, _ = A("wmodbuf1", [128, D // 2], BF16)
    wmodbufW = None
    A_sb, o_Asb = A("A_sb", [128, D], BF16)
    o_A2 = A("A2pad", [128, D], BF16)[1]
    wmodbuf = [A("wmbW0", [128, D], BF16, at=o_wmb)[0],
               A("wmbW1", [128, D], BF16, at=o_A2)[0]]
    modrow, o_modrow = A("modrow", [1, D], F32)
    modrow2, _ = A("modrow2", [1, D], F32)
    crow_sb, _ = A("crow_sb", [1, D], F32, at=o_Asb)
    scsil, _ = A("scsil", [1, D], BF16, at=o_modrow)
    ATch2 = [A(f"ATch{i}", [128, 16, 128], BF16, at=o_camul + i * 4096)[0]
             for i in range(2)]
    HT, _ = A("HT", [128, 16, D], BF16, at=o_Hot)
    ATbuf = [A(f"ATbuf{i}", [128, 16, 512], BF16, at=o_camul + i * 16384)[0]
             for i in range(2)]
    xtile = [A(f"xtile{i}", [128, 512], F32, at=o_t1f + i * 2048)[0]
             for i in range(2)]
    x2tile = [A(f"x2tile{i}", [128, 512], F32, at=o_t1f + 4096 + i * 2048)[0]
              for i in range(2)]

    cur[0] = PHASE0
    h2T, _ = A("h2T", [128, 16, CHUNK], BF16)
    hs2 = [A(f"hs{i}", [128, 8, CHUNK], BF16)[0] for i in range(2)]
    wd_sb, _ = A("wd_sb", [128, 8, D], BF16)
    wg_sb = [A(f"wg_sb{i}", [128, 16, 128], BF16)[0] for i in range(4)]
    wu_sb = [A(f"wu_sb{i}", [128, 16, 128], BF16)[0] for i in range(4)]
    hf = []
    o_hf0 = cur[0]
    for i in range(4):
        hf.append(A(f"hf{i}", [128, CHUNK], F32)[0])
    Uf, _ = A("Uf", [128, CHUNK], F32)
    o_ln2 = cur[0]
    af = [A(f"af{i}", [128, CHUNK], F32)[0] for i in range(4)]
    bcmp = [A(f"bcmp{i}", [128, CHUNK], BF16)[0] for i in range(6)]
    keep = [A(f"keep{i}", [128, CHUNK], BF16)[0] for i in range(4)]
    msc = [A(f"msc{i}", [128, CHUNK], BF16)[0] for i in range(2)]
    yt0, o_yt0 = A("ytile0", [128, D], BF16)
    yt1, o_yt1 = A("ytile1", [128, D], BF16)
    ytile = [yt0, yt1]
    gbrep, o_gbrep = A("gbrep", [128, D], F32)
    ln2scr, _ = A("ln2scr", [128, D], F32, at=o_gbrep)
    x2buf = [A(f"x2buf{i}", [128, D], F32, at=o_ln2 + i * 8192)[0] for i in range(3)]
    n2t1, _ = A("n2t1", [128, D], F32, at=o_ln2 + 24576)
    cbmul, _ = A("cbmul", [128, D], F32, at=o_ln2 + 32768)
    sbrep, _ = A("sbrep", [128, D], F32, at=o_ln2 + 40960)
    n2bf, _ = A("n2bf", [128, D], BF16, at=o_hf0)

    ps = [nc.alloc_psum_tensor(f"ps{i}", [128, D], F32) for i in range(2)]
    tp = [ps[1][:, 0:256].bitcast(BF16).rearrange("p (a b) -> p a b", a=4),
          ps[1][:, 512:768].bitcast(BF16).rearrange("p (a b) -> p a b", a=4)]
    tpdum = [ps[1][:, 256:258], ps[1][:, 768:770]]

    def fence_tp(e, tpi):
        return e.matmul(tpdum[tpi], id_sb[0:1, 0:128], id_sb[0:1, 0:2],
                        start=True, stop=True)

    p = Plan()

    def scol(setidx, q):
        c = setidx * 30 + q
        return stats[:, c:c + 1]

    def scols(setidx, q, w):
        c = setidx * 30 + q
        return stats[:, c:c + w]

    r_stats = [[Res() for _ in range(8)] for _ in range(4)]
    r_x = [Res() for _ in range(4)]
    r_H = Res()
    r_mod = Res()
    r_mod2 = Res()
    r_modscr = Res()
    r_scrow = Res()
    r_sccol = Res()
    r_gacol = Res()
    r_camul, r_sarep, r_t1 = Res(), Res(), Res()
    r_wmod = [Res(), Res()]
    r_crow, r_scsil = Res(), Res()
    r_id = Res()
    r_ps0, r_ps1 = Res(), Res()
    r_tp = [Res(), Res()]
    r_Asb = Res()
    r_ATch2 = [Res(), Res()]
    r_atscr = Res()
    r_zc = [Res() for _ in range(4)]

    zZ, zZi, zgaZ, znegm = (stats[:, 120 + i:121 + i] for i in range(4))

    p.dma("y", "ident", lambda e: e.dma_start(out=id_sb[:], in_=ident[:, :]),
          writes=[r_id])
    p.dma("y", "crow", lambda e: e.dma_start(out=crow_sb[:], in_=crow[:, :]),
          writes=[r_crow])
    p.op("s", lambda e: e.activation(scsil[:], crow_sb[:], AF.Silu),
         reads=[r_crow], writes=[r_scsil])
    p.dma("g", "scrow", lambda e: e.dma_start(out=scrow[:, :], in_=scsil[:, :]),
          reads=[r_scsil], writes=[r_scrow])
    for k in range(16):
        p.dma("g", "sccol", lambda e, k=k: e.dma_start(
            out=sc_col[:, k:k + 1],
            in_=scrow[0, 128 * k:128 * (k + 1)].rearrange("(p o) -> p o", o=1)),
            reads=[r_scrow], writes=[r_sccol] if k == 15 else [])

    r_modps = Res()
    modps = [ps[1][0:1, 1024:1536], ps[1][0:1, 1536:2048]]
    sgnrow = t1f[0:1, :]

    def mod_mms_wide(sc, pst, wtoks):
        for k in range(16):
            par = k % 2
            p.dma("g", f"wm{par}", lambda e, k=k, par=par, sc=sc: e.dma_start(
                out=wmodbuf[par][:],
                in_=wmod[128 * k:128 * (k + 1), D * sc:D * (sc + 1)]),
                writes=[r_wmod[par]])

            def mm_mod(e, k=k, par=par, pst=pst):
                for n in range(4):
                    ins = e.matmul(pst[0:1, 512 * n:512 * (n + 1)],
                                   sc_col[:, k:k + 1],
                                   wmodbuf[par][:, 512 * n:512 * (n + 1)],
                                   start=(k == 0), stop=(k == 15))
                return ins
            p.op("t", mm_mod, reads=[r_sccol, r_wmod[par]], writes=wtoks)

    def mod_copy_wide(pst, rtoks):
        p.op("s", lambda e, pst=pst: e.activation(modrow[:], pst[0:1, :], AF.Identity),
             reads=rtoks, writes=[r_mod])

    def mod_mms_narrow(sc):
        for h in range(2):
            for k in range(16):
                par = k % 2
                c0 = D * sc + 1024 * h
                p.dma("g", f"wm{par}", lambda e, k=k, par=par, c0=c0: e.dma_start(
                    out=wmodbuf[par][:, 0:1024],
                    in_=wmod[128 * k:128 * (k + 1), c0:c0 + 1024]),
                    writes=[r_wmod[par]])

                def mm_mod(e, k=k, par=par):
                    for n in range(2):
                        ins = e.matmul(modps[n],
                                       sc_col[:, k:k + 1],
                                       wmodbuf[par][:, 512 * n:512 * (n + 1)],
                                       start=(k == 0), stop=(k == 15))
                    return ins
                p.op("t", mm_mod, reads=[r_sccol, r_wmod[par]], writes=[r_modps])
            p.op("s", lambda e, h=h: e.activation(
                modrow[0:1, 1024 * h:1024 * (h + 1)], ps[1][0:1, 1024:2048],
                AF.Identity),
                reads=[r_modps], writes=[r_mod])

    def mod_post(sc):
        if sc in (1, 4):
            p.op("v", lambda e: e.tensor_scalar(modrow2[:], modrow[:], 1.0, None, ALU.add),
                 reads=[r_mod], writes=[r_mod2])
            src_, r_src = modrow2, r_mod2
        elif sc in (2, 5):
            p.op("v", lambda e: e.tensor_scalar(
                modrow2[:], modrow[:], 0.0, 2.0, ALU.is_ge, ALU.mult),
                reads=[r_mod], writes=[r_mod2])
            p.op("v", lambda e: e.tensor_scalar(
                sgnrow, modrow2[:], -1.0, None, ALU.add),
                reads=[r_mod2], writes=[r_t1])
            p.op("v", lambda e: e.scalar_tensor_tensor(
                modrow2[:], modrow[:], -1.0, modrow[:], ALU.mult, ALU.max),
                reads=[r_mod], writes=[r_mod2])
            p.op("v", lambda e: e.scalar_tensor_tensor(
                modrow[:], modrow2[:], 0.25, sgnrow, ALU.max, ALU.mult),
                reads=[r_mod2, r_t1], writes=[r_mod])
            src_, r_src = modrow, r_mod
        else:
            src_, r_src = modrow, r_mod
        p.dma("g", "modw", lambda e, sc=sc, src_=src_: e.dma_start(
            out=modscr[sc:sc + 1, :], in_=src_[:, :]),
            reads=[r_src], writes=[r_modscr])

    def mod_chunk(sc):
        mod_mms_narrow(sc)
        mod_post(sc)

    mod_mms_wide(1, ps[0], [r_ps0])
    mod_mms_wide(0, ps[1], [r_modps, r_tp[0], r_tp[1]])

    def modulate(ct):
        Hsl = (Hon if ct < 16 else Hot)[:, ct % 16, :]
        p.op("v", lambda e, Hsl=Hsl: e.tensor_tensor(t1f[:], Hsl, camul[:], ALU.mult),
             reads=[r_H, r_camul], writes=[r_t1])
        p.op("v", lambda e, Hsl=Hsl: e.tensor_tensor(Hsl, t1f[:], sarep[:], ALU.add),
             reads=[r_t1, r_sarep], writes=[r_H])

    for ct in range(NT_ALL):
        par = ct % 3
        Hdst = (Hon if ct < 16 else Hot)[:, ct % 16, :]
        rs = r_stats[par]
        p.dma("y", f"xb{par}", lambda e, ct=ct, par=par: e.dma_start(
            out=xbuf[par][:], in_=xarr[128 * ct:128 * (ct + 1), :]),
            writes=[r_x[par]])
        for q4 in range(4):
            p.op("v", lambda e, par=par, q4=q4: e.bn_stats(
                scols(par, 6 * q4, 6), xbuf[par][:, 512 * q4:512 * (q4 + 1)]),
                reads=[r_x[par]], writes=[rs[0]])
        p.op("v", lambda e, par=par: e.bn_aggr(scols(par, 24, 2), scols(par, 0, 24)),
             reads=[rs[0]], writes=[rs[1]])
        p.op("v", lambda e, par=par: e.tensor_scalar(
            scol(par, 26), scol(par, 25), 1.0, EPS, ALU.mult, ALU.add),
            reads=[rs[1]], writes=[rs[2]])
        p.op("s", lambda e, par=par: e.activation(scol(par, 27), scol(par, 26), AF.Ln),
             reads=[rs[2]], writes=[rs[3]])
        p.op("s", lambda e, par=par: e.activation(scol(par, 28), scol(par, 27), AF.Exp, scale=-0.5),
             reads=[rs[3]], writes=[rs[4]])
        p.op("v", lambda e, par=par: e.scalar_tensor_tensor(
            scol(par, 29), scol(par, 24), -1.0, scol(par, 28), ALU.mult, ALU.mult),
            reads=[rs[1], rs[4]], writes=[rs[5]])
        p.op("s", lambda e, ct=ct, par=par, Hdst=Hdst: e.activation(
            Hdst, xbuf[par][:], AF.Identity, bias=scol(par, 29), scale=scol(par, 28)),
            reads=[r_x[par], rs[4], rs[5]], writes=[r_H])

    mod_copy_wide(ps[0], [r_ps0])
    mod_post(1)
    p.dma("g", "bca", lambda e: e.dma_start(out=camul[:], in_=modscr[1:2, :].broadcast_to([128, D])),
          reads=[r_modscr], writes=[r_camul])
    mod_copy_wide(ps[1], [r_modps, r_tp[0], r_tp[1]])
    mod_post(0)
    p.dma("g", "bcs", lambda e: e.dma_start(out=sarep[:], in_=modscr[0:1, :].broadcast_to([128, D])),
          reads=[r_modscr], writes=[r_sarep])
    for ct in range(NT_ALL):
        modulate(ct)

    mod_mms_wide(2, ps[0], [r_ps0])
    mod_copy_wide(ps[0], [r_ps0])
    mod_post(2)
    for k in range(16):
        p.dma("g", "gacol", lambda e, k=k: e.dma_start(
            out=ga_col[:, k:k + 1],
            in_=modscr[2, 128 * k:128 * (k + 1)].rearrange("(p o) -> p o", o=1)),
            reads=[r_modscr], writes=[r_gacol] if k == 15 else [])

    if dbg:
        p.dma("g", "dbg", lambda e: e.dma_start(out=dbg_mod[:, :], in_=modscr[:, :]),
              reads=[r_modscr])
        p.dma("g", "dbg", lambda e: e.dma_start(out=dbg2[0], in_=Hon[:, 0, :]), reads=[r_H])
        p.dma("g", "dbg", lambda e: e.dma_start(out=dbg2[1], in_=camul[:]), reads=[r_camul])
        p.dma("g", "dbg", lambda e: e.dma_start(out=dbg2[2], in_=sarep[:]), reads=[r_sarep])
        p.dma("g", "dbg", lambda e: e.dma_start(out=dbg2[3][:, 0:16], in_=ga_col[:, :]),
              reads=[r_gacol])
        p.dma("g", "dbg", lambda e: e.dma_start(out=dbg2[3][:, 32:64], in_=stats[:, :]),
              reads=[r_stats[0][5], r_stats[1][5]])

    if dbg:
        p.dma("g", "dbg", lambda e: e.dma_start(out=dbg_H[0], in_=Hon[:, 0, :]), reads=[r_H])
        p.dma("g", "dbg", lambda e: e.dma_start(out=dbg_H[1], in_=Hot[:, 0, :]), reads=[r_H])

    def Hsl(k):
        return (Hon if k < 16 else Hot)[:, k % 16, :]

    pend_tp = None
    for bi in range(16):
        def mm_E(e, bi=bi):
            for k in range(NT_ALL):
                h = Hsl(k)
                for n in range(4):
                    ins = e.matmul(ps[0][:, 512 * n:512 * (n + 1)],
                                   h[:, 128 * bi:128 * (bi + 1)],
                                   h[:, 512 * n:512 * (n + 1)],
                                   start=(k == 0), stop=(k == NT_ALL - 1))
            return ins
        p.op("t", mm_E, reads=[r_H], writes=[r_ps0])
        if bi == 3:
            mod_chunk(3)
        elif bi == 6:
            mod_chunk(4)
        elif bi == 9:
            mod_chunk(5)

        if pend_tp is not None:
            pend_tp()
            pend_tp = None

        p.op("v", lambda e: e.reduce_max(znegm, ps[0][:], axis=AX.X, negate=True),
             reads=[r_ps0], writes=[r_zc[3]])
        p.op("s", lambda e: e.activation(A_sb[:], ps[0][:], AF.Exp,
                                         bias=znegm),
             reads=[r_ps0, r_zc[3]], writes=[r_Asb])
        p.op("v", lambda e: e.reduce_sum(zZ, A_sb[:], axis=AX.X),
             reads=[r_Asb], writes=[r_zc[0]])
        p.op("v", lambda e: e.reciprocal(zZi, zZ),
             reads=[r_zc[0]], writes=[r_zc[1]])
        p.op("v", lambda e, bi=bi: e.tensor_tensor(zgaZ, zZi, ga_col[:, bi:bi + 1], ALU.mult),
             reads=[r_zc[1], r_gacol], writes=[r_zc[2]])
        p.op("v", lambda e: e.tensor_scalar(A_sb[:], A_sb[:], zgaZ, None, ALU.mult),
             reads=[r_Asb, r_zc[2]], writes=[r_Asb])
        if dbg and bi == 0:
            p.dma("g", "dbg", lambda e: e.dma_start(out=dbg_A[1], in_=A_sb[:]), reads=[r_Asb])

        def make_tp(bi=bi):
            def emit_all():
                ach = ATch2[bi % 2]
                rach = r_ATch2[bi % 2]
                for g in range(4):
                    tpi = g % 2

                    def mm_tp(e, g=g, tpi=tpi, bi=bi):
                        for i in range(4):
                            e.transpose(tp[tpi][:, i, :],
                                        A_sb[:, 128 * (4 * g + i):128 * (4 * g + i + 1)],
                                        id_sb[:])
                        return fence_tp(e, tpi)
                    p.op("t", mm_tp, reads=[r_Asb, r_id], writes=[r_tp[tpi]])
                    p.op("v", lambda e, g=g, tpi=tpi, ach=ach, rach=rach: e.tensor_copy(
                        ach[:, 4 * g:4 * g + 4, :],
                        tp[tpi][:, :, :]),
                        reads=[r_tp[tpi]], writes=[rach, r_camul, r_sarep])
                p.dma("y", "atsp", lambda e, bi=bi, ach=ach: e.dma_start(
                    out=atscr[:, :, bi, :], in_=ach[:, :, :]),
                    reads=[rach], writes=[r_atscr])
            return emit_all
        pend_tp = make_tp()
        if bi == 15:
            pend_tp()
            pend_tp = None

    p.barrier()

    r_HT = Res()
    r_atbuf = [Res(), Res()]
    r_xt = [Res(), Res()]
    r_x2t = [Res(), Res()]
    r_psB = [Res(), Res()]
    r_out = [Res() for _ in range(16)]
    def emit_atb(ic):
        abuf = ic % 2
        p.dma("y", f"atb{abuf}", lambda e, ic=ic, abuf=abuf: e.dma_start(
            out=ATbuf[abuf][:, :, :],
            in_=atscr[:, :, 4 * ic:4 * (ic + 1), :].rearrange("p jk b c -> p jk (b c)")),
            reads=[r_atscr], writes=[r_atbuf[abuf]])

    emit_atb(0)
    for ic in range(4):
        abuf = ic % 2
        for cm in range(16):
            par = cm % 2
            psB = ps[0][:, 512 * par:512 * (par + 1)]
            if ic == 0:
                for g in range(4):
                    tpi = g % 2

                    def mm_ht(e, cm=cm, g=g, tpi=tpi):
                        for i in range(4):
                            jk = 4 * g + i
                            e.transpose(tp[tpi][:, i, :],
                                        Hon[:, cm, 128 * jk:128 * (jk + 1)],
                                        id_sb[:])
                        return fence_tp(e, tpi)
                    p.op("t", mm_ht, reads=[r_H, r_id], writes=[r_tp[tpi]])
                    p.op("v", lambda e, cm=cm, g=g, tpi=tpi: e.tensor_copy(
                        HT[:, 4 * g:4 * g + 4, 128 * cm:128 * cm + 128],
                        tp[tpi][:, :, :]),
                        reads=[r_tp[tpi]], writes=[r_HT])
            p.dma("y", f"xt{par}", lambda e, ic=ic, cm=cm, par=par: e.dma_start(
                out=xtile[par][:],
                in_=xarr[128 * cm:128 * (cm + 1), 512 * ic:512 * (ic + 1)]),
                writes=[r_xt[par]])

            def mm_out(e, ic=ic, cm=cm, par=par, abuf=abuf, psB=psB):
                for jk in range(16):
                    ins = e.matmul(psB, HT[:, jk, 128 * cm:128 * (cm + 1)],
                                   ATbuf[abuf][:, jk, :],
                                   start=(jk == 0), stop=(jk == 15))
                return ins
            wt = [r_psB[par]] + ([r_ps0] if ic == 0 and cm < 2 else [])
            p.op("t", mm_out, reads=[r_HT, r_atbuf[abuf]], writes=wt)
            if cm == 0 and ic < 3:
                emit_atb(ic + 1)
            p.op("v", lambda e, par=par, psB=psB: e.tensor_tensor(
                x2tile[par][:], psB, xtile[par][:], ALU.add),
                reads=[r_psB[par], r_xt[par]], writes=[r_x2t[par]])
            p.dma("y", f"x2t{par}", lambda e, ic=ic, cm=cm, par=par: e.dma_start(
                out=yout[128 * cm:128 * (cm + 1), 512 * ic:512 * (ic + 1)],
                in_=x2tile[par][:]),
                reads=[r_x2t[par]], writes=[r_out[cm]])

    p.barrier()

    r_gbrep, r_cbmul, r_sbrep = Res(), Res(), Res()
    r_x2b = [Res(), Res(), Res()]
    r_ln2s = Res()
    r_n2t1, r_n2bf = Res(), Res()
    r_h2T = Res()
    r_hs2 = [Res(), Res()]
    r_wd = Res()
    res_psF = [Res(), Res()]
    r_wg = [Res() for _ in range(4)]
    r_wu = [Res() for _ in range(4)]
    r_hf = [Res() for _ in range(4)]
    r_Uf = Res()
    r_af = [Res() for _ in range(4)]
    r_b = [Res() for _ in range(6)]
    r_k = [Res() for _ in range(4)]
    r_msc = [Res(), Res()]
    r_yt = [Res(), Res()]

    for ch in range(2):
        p.barrier()
        p.dma("g", "bcf2", lambda e: e.dma_start(out=cbmul[:], in_=modscr[4:5, :].broadcast_to([128, D])),
              reads=[r_modscr], writes=[r_cbmul])
        p.dma("g", "bcf3", lambda e: e.dma_start(out=sbrep[:], in_=modscr[3:4, :].broadcast_to([128, D])),
              reads=[r_modscr], writes=[r_sbrep])
        for t8 in range(8):
            par = t8 % 3
            rs = r_stats[par]
            row0 = CHUNK * ch + 128 * t8
            p.dma("y", f"x2b{par}", lambda e, row0=row0, par=par: e.dma_start(
                out=x2buf[par][:], in_=yout[row0:row0 + 128, :]),
                reads=[r_out[(CHUNK * ch + 128 * t8) // 128]],
                writes=[r_x2b[par]])
            for q4 in range(4):
                p.op("v", lambda e, par=par, q4=q4: e.bn_stats(
                    scols(par, 6 * q4, 6), x2buf[par][:, 512 * q4:512 * (q4 + 1)]),
                    reads=[r_x2b[par]], writes=[rs[0]])
            p.op("v", lambda e, par=par: e.bn_aggr(scols(par, 24, 2), scols(par, 0, 24)),
                 reads=[rs[0]], writes=[rs[1]])
            p.op("v", lambda e, par=par: e.tensor_scalar(
                scol(par, 26), scol(par, 25), 1.0, EPS, ALU.mult, ALU.add),
                reads=[rs[1]], writes=[rs[2]])
            p.op("s", lambda e, par=par: e.activation(scol(par, 27), scol(par, 26), AF.Ln),
                 reads=[rs[2]], writes=[rs[3]])
            p.op("s", lambda e, par=par: e.activation(scol(par, 28), scol(par, 27), AF.Exp, scale=-0.5),
                 reads=[rs[3]], writes=[rs[4]])
            p.op("v", lambda e, par=par: e.scalar_tensor_tensor(
                scol(par, 29), scol(par, 24), -1.0, scol(par, 28), ALU.mult, ALU.mult),
                reads=[rs[1], rs[4]], writes=[rs[5]])
            p.op("s", lambda e, par=par: e.activation(
                n2t1[:], x2buf[par][:], AF.Identity, bias=scol(par, 29), scale=scol(par, 28)),
                reads=[r_x2b[par], rs[4], rs[5]], writes=[r_n2t1])
            p.op("v", lambda e: e.tensor_tensor(ln2scr[:], n2t1[:], cbmul[:], ALU.mult),
                 reads=[r_n2t1, r_cbmul], writes=[r_ln2s])
            p.op("v", lambda e: e.tensor_tensor(n2bf[:], ln2scr[:], sbrep[:], ALU.add),
                 reads=[r_ln2s, r_sbrep], writes=[r_n2bf])
            for g in range(4):
                tpi = g % 2

                def mm_n2(e, g=g, tpi=tpi):
                    for i in range(4):
                        dk = 4 * g + i
                        e.transpose(tp[tpi][:, i, :],
                                    n2bf[:, 128 * dk:128 * (dk + 1)],
                                    id_sb[:])
                    return fence_tp(e, tpi)
                p.op("t", mm_n2, reads=[r_n2bf, r_id], writes=[r_tp[tpi]])
                p.op("v", lambda e, g=g, tpi=tpi, t8=t8: e.tensor_copy(
                    h2T[:, 4 * g:4 * g + 4, 128 * t8:128 * t8 + 128],
                    tp[tpi][:, :, :]),
                    reads=[r_tp[tpi]], writes=[r_h2T])

        p.barrier()

        pairs = [(0, 1), (0, 2), (0, 3), (1, 2), (1, 3), (2, 3)]
        cmp_after = {1: [0], 2: [1, 3], 3: [2, 4, 5]}

        def load_wgwu(cg, tau):
            j, k = tau // 4, tau % 4
            wpar = tau % 4
            p.dma("g", f"wg{wpar}", lambda e, cg=cg, j=j, k=k, wpar=wpar: e.dma_start(
                out=wg_sb[wpar][:, :, :], in_=wgP[cg, j, k]),
                writes=[r_wg[wpar]])
            p.dma("g", f"wu{wpar}", lambda e, cg=cg, j=j, k=k, wpar=wpar: e.dma_start(
                out=wu_sb[wpar][:, :, :], in_=wuP[cg, j, k]),
                writes=[r_wu[wpar]])

        def emit_wd(cg):
            p.dma("g", "wd", lambda e, cg=cg: e.dma_start(
                out=wd_sb[:, :, :],
                in_=wdP[cg].rearrange("j k p c -> p (j k) c")),
                writes=[r_wd])

        def ph1(cg):
            hsl, r_hsl = hs2[cg % 2], r_hs2[cg % 2]
            for j in range(2):
                for k in range(4):
                    tau = 4 * j + k
                    wpar = tau % 4
                    if tau >= 4:
                        load_wgwu(cg, tau)

                    def mm_G(e, wpar=wpar):
                        for dk in range(16):
                            for n in range(2):
                                ins = e.matmul(ps[1][:, 512 * n:512 * (n + 1)],
                                               wg_sb[wpar][:, dk, :],
                                               h2T[:, dk, 512 * n:512 * (n + 1)],
                                               start=(dk == 0), stop=(dk == 15))
                        return ins
                    p.op("t", mm_G, reads=[r_wg[wpar], r_h2T], writes=[res_psF[1]])

                    def mm_U(e, wpar=wpar):
                        for dk in range(16):
                            for n in range(2):
                                ins = e.matmul(ps[1][:, CHUNK + 512 * n:CHUNK + 512 * (n + 1)],
                                               wu_sb[wpar][:, dk, :],
                                               h2T[:, dk, 512 * n:512 * (n + 1)],
                                               start=(dk == 0), stop=(dk == 15))
                        return ins
                    p.op("t", mm_U, reads=[r_wu[wpar], r_h2T], writes=[res_psF[1]])
                    p.op("s", lambda e, k=k: e.activation(
                        hf[k][:], ps[1][:, 0:CHUNK], AF.Silu),
                        reads=[res_psF[1]], writes=[r_hf[k]])
                    p.op("v", lambda e: e.tensor_copy(Uf[:], ps[1][:, CHUNK:2 * CHUNK]),
                         reads=[res_psF[1]], writes=[r_Uf])
                    p.op("v", lambda e, k=k: e.tensor_tensor(af[k][:], hf[k][:], Uf[:], ALU.mult),
                         reads=[r_hf[k], r_Uf], writes=[r_af[k]])
                    p.op("s", lambda e, k=k: e.activation(hf[k][:], af[k][:], AF.Abs),
                         reads=[r_af[k]], writes=[r_hf[k]])
                    for bi_ in cmp_after.get(k, []):
                        a_, b_ = pairs[bi_]
                        p.op("v", lambda e, bi_=bi_, a_=a_, b_=b_: e.tensor_tensor(
                            bcmp[bi_][:], hf[a_][:], hf[b_][:], ALU.is_ge),
                            reads=[r_hf[a_], r_hf[b_]], writes=[r_b[bi_]])
                p.op("v", lambda e: e.tensor_tensor(msc[0][:], bcmp[0][:], bcmp[1][:], ALU.add),
                     reads=[r_b[0], r_b[1]], writes=[r_msc[0]])
                p.op("v", lambda e: e.tensor_tensor(msc[1][:], msc[0][:], bcmp[2][:], ALU.add),
                     reads=[r_msc[0], r_b[2]], writes=[r_msc[1]])
                p.op("v", lambda e: e.tensor_scalar(keep[0][:], msc[1][:], 1.5, None, ALU.is_ge),
                     reads=[r_msc[1]], writes=[r_k[0]])
                p.op("v", lambda e: e.tensor_tensor(msc[0][:], bcmp[3][:], bcmp[4][:], ALU.add),
                     reads=[r_b[3], r_b[4]], writes=[r_msc[0]])
                p.op("v", lambda e: e.tensor_tensor(msc[1][:], msc[0][:], bcmp[0][:], ALU.subtract),
                     reads=[r_msc[0], r_b[0]], writes=[r_msc[1]])
                p.op("v", lambda e: e.tensor_scalar(keep[1][:], msc[1][:], 0.5, None, ALU.is_ge),
                     reads=[r_msc[1]], writes=[r_k[1]])
                p.op("v", lambda e: e.tensor_tensor(msc[0][:], bcmp[1][:], bcmp[3][:], ALU.add),
                     reads=[r_b[1], r_b[3]], writes=[r_msc[0]])
                p.op("v", lambda e: e.tensor_tensor(msc[1][:], bcmp[5][:], msc[0][:], ALU.subtract),
                     reads=[r_msc[0], r_b[5]], writes=[r_msc[1]])
                p.op("v", lambda e: e.tensor_scalar(keep[2][:], msc[1][:], -0.5, None, ALU.is_ge),
                     reads=[r_msc[1]], writes=[r_k[2]])
                p.op("v", lambda e: e.tensor_tensor(msc[0][:], keep[0][:], keep[1][:], ALU.add),
                     reads=[r_k[0], r_k[1]], writes=[r_msc[0]])
                p.op("v", lambda e: e.tensor_tensor(msc[1][:], msc[0][:], keep[2][:], ALU.add),
                     reads=[r_msc[0], r_k[2]], writes=[r_msc[1]])
                p.op("v", lambda e: e.tensor_scalar(keep[3][:], msc[1][:], -1.0, 2.0, ALU.mult, ALU.add),
                     reads=[r_msc[1]], writes=[r_k[3]])
                for k in range(4):
                    tau = 4 * j + k
                    p.op("v", lambda e, k=k, tau=tau, hsl=hsl: e.tensor_tensor(
                        hsl[:, tau, :], af[k][:], keep[k][:], ALU.mult),
                        reads=[r_af[k], r_k[k]], writes=[r_hsl])

        def ph2(cg):
            hsl, r_hsl = hs2[cg % 2], r_hs2[cg % 2]
            for mi in range(8):
                ypar = mi % 2
                psY = ps[ypar]

                def mm_y(e, mi=mi, psY=psY, hsl=hsl):
                    for tau in range(8):
                        for n in range(4):
                            ins = e.matmul(psY[:, 512 * n:512 * (n + 1)],
                                           hsl[:, tau, 128 * mi:128 * (mi + 1)],
                                           wd_sb[:, tau, 512 * n:512 * (n + 1)],
                                           start=(tau == 0), stop=(tau == 7))
                    return ins
                p.op("t", mm_y, reads=[r_hsl, r_wd], writes=[res_psF[ypar]])
                p.op("v", lambda e, ypar=ypar, psY=psY: e.tensor_tensor(
                    ytile[ypar][:], psY[:, :], gbrep[:], ALU.mult),
                    reads=[res_psF[ypar], r_gbrep], writes=[r_yt[ypar]])
                row0 = CHUNK * ch + 128 * mi
                if not skip_ffn_accum:
                    p.dma("g", f"ac{mi}", lambda e, row0=row0, ypar=ypar: e.dma_start(
                        out=yout[row0:row0 + 128, :], in_=ytile[ypar][:],
                        accum_op=ALU.add),
                        reads=[r_yt[ypar]], writes=[r_out[row0 // 128]])

        for t_ in range(4):
            load_wgwu(0, t_)
        emit_wd(0)
        p.dma("g", "bcf", lambda e: e.dma_start(out=gbrep[:], in_=modscr[5:6, :].broadcast_to([128, D])),
              reads=[r_modscr], writes=[r_gbrep])
        ph1(0)
        for t_ in range(4):
            load_wgwu(1, t_)
        if dbg and ch == 0:
            for tau_ in range(8):
                p.dma("g", "dbg", lambda e, tau_=tau_: e.dma_start(
                    out=dbg_hs[tau_], in_=hs2[0][:, tau_, :]), reads=[r_hs2[0]])
        for cg in range(8):
            if cg < 7:
                ph1(cg + 1)
            if cg < 6:
                for t_ in range(4):
                    load_wgwu(cg + 2, t_)
            ph2(cg)
            if cg < 7:
                emit_wd(cg + 1)

    p.barrier()

    sems = {}
    from contextlib import ExitStack
    with ExitStack() as ctx:
        for name in list(p.cnt):
            sems[name] = ctx.enter_context(nc.semaphore(name))
        nc.all_engine_barrier()
        for name in list(p.cnt):
            nc.gpsimd.sem_clear(sems[name])
        nc.all_engine_barrier()
        p.replay(nc, sems)
    return nc


_CACHED = None


def _get_program():
    global _CACHED
    if _CACHED is None:
        _CACHED = build_program()
    return _CACHED


def kernel(L0_attn_x, L0_ffn_w_gate, L0_ffn_w_up, L0_ffn_w_down, L0_c, L0_wmod):
    x = np.ascontiguousarray(np.asarray(L0_attn_x, dtype=np.float32))
    wg = np.ascontiguousarray(np.asarray(L0_ffn_w_gate, dtype=np.float32))
    wu = np.ascontiguousarray(np.asarray(L0_ffn_w_up, dtype=np.float32))
    wd = np.ascontiguousarray(np.asarray(L0_ffn_w_down, dtype=np.float32))
    c = np.ascontiguousarray(np.asarray(L0_c, dtype=np.float32))
    wm = np.ascontiguousarray(np.asarray(L0_wmod, dtype=np.float32))

    wgP = np.ascontiguousarray(
        wg.reshape(16, 128, 8, 2, 128, 4).transpose(2, 3, 5, 1, 0, 4))
    wuP = np.ascontiguousarray(
        wu.reshape(16, 128, 8, 2, 128, 4).transpose(2, 3, 5, 1, 0, 4))
    wdP = np.ascontiguousarray(
        wd.reshape(8, 2, 128, 4, D).transpose(0, 1, 3, 2, 4))
    ident = np.eye(128, dtype=ml_dtypes.bfloat16)

    nc = _get_program()
    in_maps = []
    for core in range(NCORES):
        b, pp = core // 2, core % 2
        own = x[b, TOK * pp:TOK * (pp + 1), :]
        oth = x[b, TOK * (1 - pp):TOK * (2 - pp), :]
        xarr = np.ascontiguousarray(np.concatenate([own, oth], axis=0))
        in_maps.append(dict(xarr=xarr, crow=c[b:b + 1, :], wmod=wm,
                            wgP=wgP, wuP=wuP, wdP=wdP, ident=ident))
    res = run_bass_kernel_spmd(nc, in_maps, core_ids=list(range(NCORES)))
    out = np.empty((B, S, D), dtype=np.float32)
    for core in range(NCORES):
        b, pp = core // 2, core % 2
        out[b, TOK * pp:TOK * (pp + 1), :] = res.results[core]["yout"]
    return out


# version 20 (final)
# speedup vs baseline: 1.0312x; 1.0171x over previous
import numpy as np
import ml_dtypes
import concourse.bass as bass
import concourse.mybir as mybir
from concourse.bass_utils import run_bass_kernel_spmd

F32 = mybir.dt.float32
BF16 = mybir.dt.bfloat16
AF = mybir.ActivationFunctionType
ALU = mybir.AluOpType
AX = mybir.AxisListType

B, S, D = 4, 4096, 2048
FI = 8192
NCORES = 8
TOK = 2048
NT_ALL = 32
NT_OWN = 16
CHUNK = 1024
EPS = 1e-5
KB = 1024


class Tk:
    __slots__ = ("sem", "val", "stream", "compute")

    def __init__(self, sem, val, stream, compute):
        self.sem, self.val, self.stream, self.compute = sem, val, stream, compute


class Res:
    __slots__ = ("w", "rs")

    def __init__(self):
        self.w = None
        self.rs = []


class Plan:
    STREAMS = ("t", "v", "s", "g", "y")
    CSEM = {"t": "t", "v": "v", "s": "s", "g": "gc"}

    def __init__(self):
        self.ops = {st: [] for st in self.STREAMS}
        self.cnt = {"t": 0, "v": 0, "s": 0, "gc": 0}
        self.dma_slots = []

    def _do(self, stream, sem, amt, emit, reads, writes, waits):
        need = list(waits)
        for r in reads:
            if r.w is not None:
                need.append(r.w)
        for w in writes:
            if w.w is not None:
                need.append(w.w)
            need.extend(w.rs)
        if sem not in self.cnt:
            self.cnt[sem] = 0
            self.dma_slots.append(sem)
        self.cnt[sem] += amt
        tok = Tk(sem, self.cnt[sem], stream, amt == 1)
        self.ops[stream].append((need, emit, sem, amt))
        for r in reads:
            r.rs.append(tok)
        for w in writes:
            w.w = tok
            w.rs = []
        return tok

    def op(self, stream, emit, reads=(), writes=(), waits=()):
        return self._do(stream, self.CSEM[stream], 1, emit, reads, writes, waits)

    def dma(self, stream, slot, emit, reads=(), writes=(), waits=()):
        assert stream in ("g", "y")
        return self._do(stream, "q_" + slot, 16, emit, reads, writes, waits)

    def barrier(self):
        toks = []
        for sem, val in self.cnt.items():
            if sem in ("t", "v", "s", "gc"):
                st = {"t": "t", "v": "v", "s": "s", "gc": "g"}[sem]
                toks.append(Tk(sem, val, st, True))
            else:
                toks.append(Tk(sem, val, "?", False))
        for st in self.STREAMS:
            need = [t for t in toks if not (t.stream == st and t.compute)]
            self.ops[st].append((need, None, None, 0))

    def replay(self, nc, semobjs):
        engines = {"t": nc.tensor, "v": nc.vector, "s": nc.scalar,
                   "g": nc.gpsimd, "y": nc.sync}
        block_ctx = nc.Block()
        with block_ctx as block:
            def mk(st):
                def run(eng):
                    hwm = {}
                    for need, emit, sem, amt in self.ops[st]:
                        for w in need:
                            if st == "t" and w.stream == "t" and w.compute:
                                continue
                            if hwm.get(w.sem, 0) >= w.val:
                                continue
                            eng.wait_ge(semobjs[w.sem], w.val)
                            hwm[w.sem] = w.val
                        if emit is not None:
                            ins = emit(eng)
                            ins.then_inc(semobjs[sem], amt)
                return run
            block.tensor(mk("t"))
            block.vector(mk("v"))
            block.scalar(mk("s"))
            block.gpsimd(mk("g"))
            block.sync(mk("y"))


def build_program(dbg=False, skip_ffn_accum=False):
    nc = bass.Bass("TRN2", target_bir_lowering=False, debug=False,
                   detect_race_conditions=False)

    xarr = nc.dram_tensor("xarr", [S, D], F32, kind="ExternalInput").ap()
    crow = nc.dram_tensor("crow", [1, D], F32, kind="ExternalInput").ap()
    wmod = nc.dram_tensor("wmod", [D, 6 * D], F32, kind="ExternalInput").ap()
    wgP = nc.dram_tensor("wgP", [8, 2, 4, 128, 16, 128], F32, kind="ExternalInput").ap()
    wuP = nc.dram_tensor("wuP", [8, 2, 4, 128, 16, 128], F32, kind="ExternalInput").ap()
    wdP = nc.dram_tensor("wdP", [8, 2, 4, 128, D], F32, kind="ExternalInput").ap()
    ident = nc.dram_tensor("ident", [128, 128], BF16, kind="ExternalInput").ap()
    yout = nc.dram_tensor("yout", [TOK, D], F32, kind="ExternalOutput").ap()

    if dbg:
        dbg_mod = nc.dram_tensor("dbg_mod", [6, D], F32, kind="ExternalOutput").ap()
        dbg_H = nc.dram_tensor("dbg_H", [2, 128, D], F32, kind="ExternalOutput").ap()
        dbg_A = nc.dram_tensor("dbg_A", [2, 128, D], F32, kind="ExternalOutput").ap()
        dbg_hs = nc.dram_tensor("dbg_hs", [8, 128, CHUNK], F32, kind="ExternalOutput").ap()
        dbg2 = nc.dram_tensor("dbg2", [4, 128, D], F32, kind="ExternalOutput").ap()

    scrow = nc.dram_tensor("scrow", [1, D], BF16, kind="Internal").ap()
    modscr = nc.dram_tensor("modscr", [6, D], F32, kind="Internal").ap()
    atscr = nc.dram_tensor("atscr", [128, 16, 16, 128], BF16, kind="Internal").ap()

    BASE = (nc.sbuf_base + 511) // 512 * 512
    TOPB = nc.sbuf_top
    _sz = {F32: 4, BF16: 2}
    cur = [BASE]

    def A(name, shape, dt, at=None):
        nbytes = int(np.prod(shape[1:])) * _sz[dt]
        if at is None:
            at = cur[0]
            cur[0] = at + (nbytes + 255) // 256 * 256
        assert at + nbytes <= TOPB, (name, hex(at), nbytes, TOPB)
        return nc.alloc_sbuf_tensor_at(name, shape, dt, offset=at), at

    id_sb, _ = A("id_sb", [128, 128], BF16)
    sc_col, _ = A("sc_col", [128, 16], BF16)
    ga_col, _ = A("ga_col", [128, 16], F32)
    stats, _ = A("stats", [128, 128], F32)
    PHASE0 = cur[0]

    Hon, _ = A("Hon", [128, 16, D], BF16)
    Hot, o_Hot = A("Hot", [128, 16, D], BF16)
    camul, o_camul = A("camul", [128, D], F32)
    sarep, _ = A("sarep", [128, D], F32)
    xbuf = [A(f"xbuf{i}", [128, D], F32)[0] for i in range(3)]
    t1f, o_t1f = A("t1f", [128, D], F32)
    wmodbuf0, o_wmb = A("wmodbuf0", [128, D // 2], BF16)
    wmodbuf1, _ = A("wmodbuf1", [128, D // 2], BF16)
    wmodbufW = None
    A_sb, o_Asb = A("A_sb", [128, D], BF16)
    o_A2 = A("A2pad", [128, D], BF16)[1]
    wmodbuf = [A("wmbW0", [128, D], BF16, at=o_wmb)[0],
               A("wmbW1", [128, D], BF16, at=o_A2)[0]]
    modrow, o_modrow = A("modrow", [1, D], F32)
    modrow2, _ = A("modrow2", [1, D], F32)
    crow_sb, _ = A("crow_sb", [1, D], F32, at=o_Asb)
    scsil, _ = A("scsil", [1, D], BF16, at=o_modrow)
    ATch2 = [A(f"ATch{i}", [128, 16, 128], BF16, at=o_camul + i * 4096)[0]
             for i in range(2)]
    HT, _ = A("HT", [128, 16, D], BF16, at=o_Hot)
    ATbuf = [A(f"ATbuf{i}", [128, 16, 512], BF16, at=o_camul + i * 16384)[0]
             for i in range(2)]
    xtile = [A(f"xtile{i}", [128, 512], F32, at=o_t1f + i * 2048)[0]
             for i in range(2)]
    x2tile = [A(f"x2tile{i}", [128, 512], F32, at=o_t1f + 4096 + i * 2048)[0]
              for i in range(2)]

    cur[0] = PHASE0
    h2T, _ = A("h2T", [128, 16, CHUNK], BF16)
    hs2 = [A(f"hs{i}", [128, 8, CHUNK], BF16)[0] for i in range(2)]
    wd_sb, _ = A("wd_sb", [128, 8, D], BF16)
    wg_sb = [A(f"wg_sb{i}", [128, 16, 128], BF16)[0] for i in range(4)]
    wu_sb = [A(f"wu_sb{i}", [128, 16, 128], BF16)[0] for i in range(4)]
    hf = []
    o_hf0 = cur[0]
    for i in range(4):
        hf.append(A(f"hf{i}", [128, CHUNK], F32)[0])
    Uf, _ = A("Uf", [128, CHUNK], F32)
    o_ln2 = cur[0]
    af = [A(f"af{i}", [128, CHUNK], F32)[0] for i in range(4)]
    bcmp = [A(f"bcmp{i}", [128, CHUNK], BF16)[0] for i in range(6)]
    keep = [A(f"keep{i}", [128, CHUNK], BF16)[0] for i in range(4)]
    msc = [A(f"msc{i}", [128, CHUNK], BF16)[0] for i in range(2)]
    yt0, o_yt0 = A("ytile0", [128, D], BF16)
    yt1, o_yt1 = A("ytile1", [128, D], BF16)
    ytile = [yt0, yt1]
    gbrep, o_gbrep = A("gbrep", [128, D], F32)
    ln2scr, _ = A("ln2scr", [128, D], F32, at=o_gbrep)
    x2buf = [A(f"x2buf{i}", [128, D], F32, at=o_ln2 + i * 8192)[0] for i in range(3)]
    n2t1, _ = A("n2t1", [128, D], F32, at=o_ln2 + 24576)
    cbmul, _ = A("cbmul", [128, D], F32, at=o_ln2 + 32768)
    sbrep, _ = A("sbrep", [128, D], F32, at=o_ln2 + 40960)
    n2bf, _ = A("n2bf", [128, D], BF16, at=o_hf0)

    ps = [nc.alloc_psum_tensor(f"ps{i}", [128, D], F32) for i in range(2)]
    tp = [ps[1][:, 0:256].bitcast(BF16).rearrange("p (a b) -> p a b", a=4),
          ps[1][:, 512:768].bitcast(BF16).rearrange("p (a b) -> p a b", a=4)]
    tpdum = [ps[1][:, 256:258], ps[1][:, 768:770]]

    def fence_tp(e, tpi):
        return e.matmul(tpdum[tpi], id_sb[0:1, 0:128], id_sb[0:1, 0:2],
                        start=True, stop=True)

    p = Plan()

    def scol(setidx, q):
        c = setidx * 30 + q
        return stats[:, c:c + 1]

    def scols(setidx, q, w):
        c = setidx * 30 + q
        return stats[:, c:c + w]

    r_stats = [[Res() for _ in range(8)] for _ in range(4)]
    r_x = [Res() for _ in range(4)]
    r_H = Res()
    r_mod = Res()
    r_mod2 = Res()
    r_modscr = Res()
    r_scrow = Res()
    r_sccol = Res()
    r_gacol = Res()
    r_camul, r_sarep, r_t1 = Res(), Res(), Res()
    r_wmod = [Res(), Res()]
    r_crow, r_scsil = Res(), Res()
    r_id = Res()
    r_ps0, r_ps1 = Res(), Res()
    r_tp = [Res(), Res()]
    r_Asb = Res()
    r_ATch2 = [Res(), Res()]
    r_atscr = Res()
    r_zc = [Res() for _ in range(4)]

    zZ, zZi, zgaZ, znegm = (stats[:, 120 + i:121 + i] for i in range(4))

    p.dma("y", "ident", lambda e: e.dma_start(out=id_sb[:], in_=ident[:, :]),
          writes=[r_id])
    p.dma("y", "crow", lambda e: e.dma_start(out=crow_sb[:], in_=crow[:, :]),
          writes=[r_crow])
    p.op("s", lambda e: e.activation(scsil[:], crow_sb[:], AF.Silu),
         reads=[r_crow], writes=[r_scsil])
    p.dma("g", "scrow", lambda e: e.dma_start(out=scrow[:, :], in_=scsil[:, :]),
          reads=[r_scsil], writes=[r_scrow])
    for k in range(16):
        p.dma("g", "sccol", lambda e, k=k: e.dma_start(
            out=sc_col[:, k:k + 1],
            in_=scrow[0, 128 * k:128 * (k + 1)].rearrange("(p o) -> p o", o=1)),
            reads=[r_scrow], writes=[r_sccol])

    r_modps = Res()
    modps = [ps[1][0:1, 1024:1536], ps[1][0:1, 1536:2048]]
    sgnrow = t1f[0:1, :]

    def mod_mms_wide(sc, pst, wtoks):
        for k in range(16):
            par = k % 2
            p.dma("g", f"wm{par}", lambda e, k=k, par=par, sc=sc: e.dma_start(
                out=wmodbuf[par][:],
                in_=wmod[128 * k:128 * (k + 1), D * sc:D * (sc + 1)]),
                writes=[r_wmod[par]])

            def mm_mod(e, k=k, par=par, pst=pst):
                for n in range(4):
                    ins = e.matmul(pst[0:1, 512 * n:512 * (n + 1)],
                                   sc_col[:, k:k + 1],
                                   wmodbuf[par][:, 512 * n:512 * (n + 1)],
                                   start=(k == 0), stop=(k == 15))
                return ins
            p.op("t", mm_mod, reads=[r_sccol, r_wmod[par]], writes=wtoks)

    def mod_copy_wide(pst, rtoks):
        p.op("s", lambda e, pst=pst: e.activation(modrow[:], pst[0:1, :], AF.Identity),
             reads=rtoks, writes=[r_mod])

    def mod_mms_narrow(sc):
        for h in range(2):
            for k in range(16):
                par = k % 2
                c0 = D * sc + 1024 * h
                p.dma("g", f"wm{par}", lambda e, k=k, par=par, c0=c0: e.dma_start(
                    out=wmodbuf[par][:, 0:1024],
                    in_=wmod[128 * k:128 * (k + 1), c0:c0 + 1024]),
                    writes=[r_wmod[par]])

                def mm_mod(e, k=k, par=par):
                    for n in range(2):
                        ins = e.matmul(modps[n],
                                       sc_col[:, k:k + 1],
                                       wmodbuf[par][:, 512 * n:512 * (n + 1)],
                                       start=(k == 0), stop=(k == 15))
                    return ins
                p.op("t", mm_mod, reads=[r_sccol, r_wmod[par]], writes=[r_modps])
            p.op("s", lambda e, h=h: e.activation(
                modrow[0:1, 1024 * h:1024 * (h + 1)], ps[1][0:1, 1024:2048],
                AF.Identity),
                reads=[r_modps], writes=[r_mod])

    def mod_post(sc):
        if sc in (1, 4):
            p.op("v", lambda e: e.tensor_scalar(modrow2[:], modrow[:], 1.0, None, ALU.add),
                 reads=[r_mod], writes=[r_mod2])
            src_, r_src = modrow2, r_mod2
        elif sc in (2, 5):
            p.op("v", lambda e: e.tensor_scalar(
                modrow2[:], modrow[:], 0.0, 2.0, ALU.is_ge, ALU.mult),
                reads=[r_mod], writes=[r_mod2])
            p.op("v", lambda e: e.tensor_scalar(
                sgnrow, modrow2[:], -1.0, None, ALU.add),
                reads=[r_mod2], writes=[r_t1])
            p.op("v", lambda e: e.scalar_tensor_tensor(
                modrow2[:], modrow[:], -1.0, modrow[:], ALU.mult, ALU.max),
                reads=[r_mod], writes=[r_mod2])
            p.op("v", lambda e: e.scalar_tensor_tensor(
                modrow[:], modrow2[:], 0.25, sgnrow, ALU.max, ALU.mult),
                reads=[r_mod2, r_t1], writes=[r_mod])
            src_, r_src = modrow, r_mod
        else:
            src_, r_src = modrow, r_mod
        p.dma("g", "modw", lambda e, sc=sc, src_=src_: e.dma_start(
            out=modscr[sc:sc + 1, :], in_=src_[:, :]),
            reads=[r_src], writes=[r_modscr])

    def mod_chunk(sc):
        mod_mms_narrow(sc)
        mod_post(sc)

    mod_mms_wide(1, ps[0], [r_ps0])
    mod_mms_wide(0, ps[1], [r_modps, r_tp[0], r_tp[1]])

    def modulate(ct):
        Hsl = (Hon if ct < 16 else Hot)[:, ct % 16, :]
        p.op("v", lambda e, Hsl=Hsl: e.tensor_tensor(t1f[:], Hsl, camul[:], ALU.mult),
             reads=[r_H, r_camul], writes=[r_t1])
        p.op("v", lambda e, Hsl=Hsl: e.tensor_tensor(Hsl, t1f[:], sarep[:], ALU.add),
             reads=[r_t1, r_sarep], writes=[r_H])

    for ct in range(NT_ALL):
        par = ct % 3
        Hdst = (Hon if ct < 16 else Hot)[:, ct % 16, :]
        rs = r_stats[par]
        p.dma("y", f"xb{par}", lambda e, ct=ct, par=par: e.dma_start(
            out=xbuf[par][:], in_=xarr[128 * ct:128 * (ct + 1), :]),
            writes=[r_x[par]])
        for q4 in range(4):
            p.op("v", lambda e, par=par, q4=q4: e.bn_stats(
                scols(par, 6 * q4, 6), xbuf[par][:, 512 * q4:512 * (q4 + 1)]),
                reads=[r_x[par]], writes=[rs[0]])
        p.op("v", lambda e, par=par: e.bn_aggr(scols(par, 24, 2), scols(par, 0, 24)),
             reads=[rs[0]], writes=[rs[1]])
        p.op("v", lambda e, par=par: e.tensor_scalar(
            scol(par, 26), scol(par, 25), 1.0, EPS, ALU.mult, ALU.add),
            reads=[rs[1]], writes=[rs[2]])
        p.op("s", lambda e, par=par: e.activation(scol(par, 27), scol(par, 26), AF.Ln),
             reads=[rs[2]], writes=[rs[3]])
        p.op("s", lambda e, par=par: e.activation(scol(par, 28), scol(par, 27), AF.Exp, scale=-0.5),
             reads=[rs[3]], writes=[rs[4]])
        p.op("v", lambda e, par=par: e.scalar_tensor_tensor(
            scol(par, 29), scol(par, 24), -1.0, scol(par, 28), ALU.mult, ALU.mult),
            reads=[rs[1], rs[4]], writes=[rs[5]])
        p.op("s", lambda e, ct=ct, par=par, Hdst=Hdst: e.activation(
            Hdst, xbuf[par][:], AF.Identity, bias=scol(par, 29), scale=scol(par, 28)),
            reads=[r_x[par], rs[4], rs[5]], writes=[r_H])

    mod_copy_wide(ps[0], [r_ps0])
    mod_post(1)
    p.dma("g", "bca", lambda e: e.dma_start(out=camul[:], in_=modscr[1:2, :].broadcast_to([128, D])),
          reads=[r_modscr], writes=[r_camul])
    mod_copy_wide(ps[1], [r_modps, r_tp[0], r_tp[1]])
    mod_post(0)
    p.dma("g", "bcs", lambda e: e.dma_start(out=sarep[:], in_=modscr[0:1, :].broadcast_to([128, D])),
          reads=[r_modscr], writes=[r_sarep])
    for ct in range(NT_ALL):
        modulate(ct)

    mod_mms_wide(2, ps[0], [r_ps0])
    mod_copy_wide(ps[0], [r_ps0])
    mod_post(2)
    for k in range(16):
        p.dma("g", "gacol", lambda e, k=k: e.dma_start(
            out=ga_col[:, k:k + 1],
            in_=modscr[2, 128 * k:128 * (k + 1)].rearrange("(p o) -> p o", o=1)),
            reads=[r_modscr], writes=[r_gacol])

    if dbg:
        p.dma("g", "dbg", lambda e: e.dma_start(out=dbg_mod[:, :], in_=modscr[:, :]),
              reads=[r_modscr])
        p.dma("g", "dbg", lambda e: e.dma_start(out=dbg2[0], in_=Hon[:, 0, :]), reads=[r_H])
        p.dma("g", "dbg", lambda e: e.dma_start(out=dbg2[1], in_=camul[:]), reads=[r_camul])
        p.dma("g", "dbg", lambda e: e.dma_start(out=dbg2[2], in_=sarep[:]), reads=[r_sarep])
        p.dma("g", "dbg", lambda e: e.dma_start(out=dbg2[3][:, 0:16], in_=ga_col[:, :]),
              reads=[r_gacol])
        p.dma("g", "dbg", lambda e: e.dma_start(out=dbg2[3][:, 32:64], in_=stats[:, :]),
              reads=[r_stats[0][5], r_stats[1][5]])

    if dbg:
        p.dma("g", "dbg", lambda e: e.dma_start(out=dbg_H[0], in_=Hon[:, 0, :]), reads=[r_H])
        p.dma("g", "dbg", lambda e: e.dma_start(out=dbg_H[1], in_=Hot[:, 0, :]), reads=[r_H])

    def Hsl(k):
        return (Hon if k < 16 else Hot)[:, k % 16, :]

    pend_tp = None
    for bi in range(16):
        def mm_E(e, bi=bi):
            for k in range(NT_ALL):
                h = Hsl(k)
                for n in range(4):
                    ins = e.matmul(ps[0][:, 512 * n:512 * (n + 1)],
                                   h[:, 128 * bi:128 * (bi + 1)],
                                   h[:, 512 * n:512 * (n + 1)],
                                   start=(k == 0), stop=(k == NT_ALL - 1))
            return ins
        p.op("t", mm_E, reads=[r_H], writes=[r_ps0])
        if bi == 3:
            mod_chunk(3)
        elif bi == 6:
            mod_chunk(4)
        elif bi == 9:
            mod_chunk(5)

        if pend_tp is not None:
            pend_tp()
            pend_tp = None

        p.op("v", lambda e: e.reduce_max(znegm, ps[0][:], axis=AX.X, negate=True),
             reads=[r_ps0], writes=[r_zc[3]])
        p.op("s", lambda e: e.activation(A_sb[:], ps[0][:], AF.Exp,
                                         bias=znegm),
             reads=[r_ps0, r_zc[3]], writes=[r_Asb])
        p.op("v", lambda e: e.reduce_sum(zZ, A_sb[:], axis=AX.X),
             reads=[r_Asb], writes=[r_zc[0]])
        p.op("v", lambda e: e.reciprocal(zZi, zZ),
             reads=[r_zc[0]], writes=[r_zc[1]])
        p.op("v", lambda e, bi=bi: e.tensor_tensor(zgaZ, zZi, ga_col[:, bi:bi + 1], ALU.mult),
             reads=[r_zc[1], r_gacol], writes=[r_zc[2]])
        p.op("v", lambda e: e.tensor_scalar(A_sb[:], A_sb[:], zgaZ, None, ALU.mult),
             reads=[r_Asb, r_zc[2]], writes=[r_Asb])
        if dbg and bi == 0:
            p.dma("g", "dbg", lambda e: e.dma_start(out=dbg_A[1], in_=A_sb[:]), reads=[r_Asb])

        def make_tp(bi=bi):
            def emit_all():
                ach = ATch2[bi % 2]
                rach = r_ATch2[bi % 2]
                for g in range(4):
                    tpi = g % 2

                    def mm_tp(e, g=g, tpi=tpi, bi=bi):
                        for i in range(4):
                            e.transpose(tp[tpi][:, i, :],
                                        A_sb[:, 128 * (4 * g + i):128 * (4 * g + i + 1)],
                                        id_sb[:])
                        return fence_tp(e, tpi)
                    p.op("t", mm_tp, reads=[r_Asb, r_id], writes=[r_tp[tpi]])
                    p.op("v", lambda e, g=g, tpi=tpi, ach=ach, rach=rach: e.tensor_copy(
                        ach[:, 4 * g:4 * g + 4, :],
                        tp[tpi][:, :, :]),
                        reads=[r_tp[tpi]], writes=[rach, r_camul, r_sarep])
                p.dma("y", "atsp", lambda e, bi=bi, ach=ach: e.dma_start(
                    out=atscr[:, :, bi, :], in_=ach[:, :, :]),
                    reads=[rach], writes=[r_atscr])
            return emit_all
        pend_tp = make_tp()
        if bi == 15:
            pend_tp()
            pend_tp = None

    p.barrier()

    r_HT = Res()
    r_atbuf = [Res(), Res()]
    r_xt = [Res(), Res()]
    r_x2t = [Res(), Res()]
    r_psB = [Res(), Res()]
    r_out = [Res() for _ in range(16)]
    def emit_atb(ic):
        abuf = ic % 2
        p.dma("y", f"atb{abuf}", lambda e, ic=ic, abuf=abuf: e.dma_start(
            out=ATbuf[abuf][:, :, :],
            in_=atscr[:, :, 4 * ic:4 * (ic + 1), :].rearrange("p jk b c -> p jk (b c)")),
            reads=[r_atscr], writes=[r_atbuf[abuf]])

    emit_atb(0)
    for ic in range(4):
        abuf = ic % 2
        for cm in range(16):
            par = cm % 2
            psB = ps[0][:, 512 * par:512 * (par + 1)]
            if ic == 0:
                for g in range(4):
                    tpi = g % 2

                    def mm_ht(e, cm=cm, g=g, tpi=tpi):
                        for i in range(4):
                            jk = 4 * g + i
                            e.transpose(tp[tpi][:, i, :],
                                        Hon[:, cm, 128 * jk:128 * (jk + 1)],
                                        id_sb[:])
                        return fence_tp(e, tpi)
                    p.op("t", mm_ht, reads=[r_H, r_id], writes=[r_tp[tpi]])
                    p.op("v", lambda e, cm=cm, g=g, tpi=tpi: e.tensor_copy(
                        HT[:, 4 * g:4 * g + 4, 128 * cm:128 * cm + 128],
                        tp[tpi][:, :, :]),
                        reads=[r_tp[tpi]], writes=[r_HT])
            p.dma("g", f"xt{par}", lambda e, ic=ic, cm=cm, par=par: e.dma_start(
                out=xtile[par][:],
                in_=xarr[128 * cm:128 * (cm + 1), 512 * ic:512 * (ic + 1)]),
                writes=[r_xt[par]])

            def mm_out(e, ic=ic, cm=cm, par=par, abuf=abuf, psB=psB):
                for jk in range(16):
                    ins = e.matmul(psB, HT[:, jk, 128 * cm:128 * (cm + 1)],
                                   ATbuf[abuf][:, jk, :],
                                   start=(jk == 0), stop=(jk == 15))
                return ins
            wt = [r_psB[par]] + ([r_ps0] if ic == 0 and cm < 2 else [])
            p.op("t", mm_out, reads=[r_HT, r_atbuf[abuf]], writes=wt)
            if cm == 0 and ic < 3:
                emit_atb(ic + 1)
            p.op("v", lambda e, par=par, psB=psB: e.tensor_tensor(
                x2tile[par][:], psB, xtile[par][:], ALU.add),
                reads=[r_psB[par], r_xt[par]], writes=[r_x2t[par]])
            p.dma("y", f"x2t{par}", lambda e, ic=ic, cm=cm, par=par: e.dma_start(
                out=yout[128 * cm:128 * (cm + 1), 512 * ic:512 * (ic + 1)],
                in_=x2tile[par][:]),
                reads=[r_x2t[par]], writes=[r_out[cm]])

    p.barrier()

    r_gbrep, r_cbmul, r_sbrep = Res(), Res(), Res()
    r_x2b = [Res(), Res(), Res()]
    r_ln2s = Res()
    r_n2t1, r_n2bf = Res(), Res()
    r_h2T = Res()
    r_hs2 = [Res(), Res()]
    r_wd = Res()
    res_psF = [Res(), Res()]
    r_wg = [Res() for _ in range(4)]
    r_wu = [Res() for _ in range(4)]
    r_hf = [Res() for _ in range(4)]
    r_Uf = Res()
    r_af = [Res() for _ in range(4)]
    r_b = [Res() for _ in range(6)]
    r_k = [Res() for _ in range(4)]
    r_msc = [Res(), Res()]
    r_yt = [Res(), Res()]

    for ch in range(2):
        p.barrier()
        p.dma("g", "bcf2", lambda e: e.dma_start(out=cbmul[:], in_=modscr[4:5, :].broadcast_to([128, D])),
              reads=[r_modscr], writes=[r_cbmul])
        p.dma("g", "bcf3", lambda e: e.dma_start(out=sbrep[:], in_=modscr[3:4, :].broadcast_to([128, D])),
              reads=[r_modscr], writes=[r_sbrep])
        for t8 in range(8):
            par = t8 % 3
            rs = r_stats[par]
            row0 = CHUNK * ch + 128 * t8
            p.dma("y", f"x2b{par}", lambda e, row0=row0, par=par: e.dma_start(
                out=x2buf[par][:], in_=yout[row0:row0 + 128, :]),
                reads=[r_out[(CHUNK * ch + 128 * t8) // 128]],
                writes=[r_x2b[par]])
            for q4 in range(4):
                p.op("v", lambda e, par=par, q4=q4: e.bn_stats(
                    scols(par, 6 * q4, 6), x2buf[par][:, 512 * q4:512 * (q4 + 1)]),
                    reads=[r_x2b[par]], writes=[rs[0]])
            p.op("v", lambda e, par=par: e.bn_aggr(scols(par, 24, 2), scols(par, 0, 24)),
                 reads=[rs[0]], writes=[rs[1]])
            p.op("v", lambda e, par=par: e.tensor_scalar(
                scol(par, 26), scol(par, 25), 1.0, EPS, ALU.mult, ALU.add),
                reads=[rs[1]], writes=[rs[2]])
            p.op("s", lambda e, par=par: e.activation(scol(par, 27), scol(par, 26), AF.Ln),
                 reads=[rs[2]], writes=[rs[3]])
            p.op("s", lambda e, par=par: e.activation(scol(par, 28), scol(par, 27), AF.Exp, scale=-0.5),
                 reads=[rs[3]], writes=[rs[4]])
            p.op("v", lambda e, par=par: e.scalar_tensor_tensor(
                scol(par, 29), scol(par, 24), -1.0, scol(par, 28), ALU.mult, ALU.mult),
                reads=[rs[1], rs[4]], writes=[rs[5]])
            p.op("s", lambda e, par=par: e.activation(
                n2t1[:], x2buf[par][:], AF.Identity, bias=scol(par, 29), scale=scol(par, 28)),
                reads=[r_x2b[par], rs[4], rs[5]], writes=[r_n2t1])
            p.op("v", lambda e: e.tensor_tensor(ln2scr[:], n2t1[:], cbmul[:], ALU.mult),
                 reads=[r_n2t1, r_cbmul], writes=[r_ln2s])
            p.op("v", lambda e: e.tensor_tensor(n2bf[:], ln2scr[:], sbrep[:], ALU.add),
                 reads=[r_ln2s, r_sbrep], writes=[r_n2bf])
            for g in range(4):
                tpi = g % 2

                def mm_n2(e, g=g, tpi=tpi):
                    for i in range(4):
                        dk = 4 * g + i
                        e.transpose(tp[tpi][:, i, :],
                                    n2bf[:, 128 * dk:128 * (dk + 1)],
                                    id_sb[:])
                    return fence_tp(e, tpi)
                p.op("t", mm_n2, reads=[r_n2bf, r_id], writes=[r_tp[tpi]])
                p.op("v", lambda e, g=g, tpi=tpi, t8=t8: e.tensor_copy(
                    h2T[:, 4 * g:4 * g + 4, 128 * t8:128 * t8 + 128],
                    tp[tpi][:, :, :]),
                    reads=[r_tp[tpi]], writes=[r_h2T])

        p.barrier()

        pairs = [(0, 1), (0, 2), (0, 3), (1, 2), (1, 3), (2, 3)]
        cmp_after = {1: [0], 2: [1, 3], 3: [2, 4, 5]}

        def load_wgwu(cg, tau):
            j, k = tau // 4, tau % 4
            wpar = tau % 4
            p.dma("g", f"wg{wpar}", lambda e, cg=cg, j=j, k=k, wpar=wpar: e.dma_start(
                out=wg_sb[wpar][:, :, :], in_=wgP[cg, j, k]),
                writes=[r_wg[wpar]])
            p.dma("g", f"wu{wpar}", lambda e, cg=cg, j=j, k=k, wpar=wpar: e.dma_start(
                out=wu_sb[wpar][:, :, :], in_=wuP[cg, j, k]),
                writes=[r_wu[wpar]])

        def emit_wd(cg):
            p.dma("g", "wd", lambda e, cg=cg: e.dma_start(
                out=wd_sb[:, :, :],
                in_=wdP[cg].rearrange("j k p c -> p (j k) c")),
                writes=[r_wd])

        def ph1(cg):
            hsl, r_hsl = hs2[cg % 2], r_hs2[cg % 2]
            for j in range(2):
                for k in range(4):
                    tau = 4 * j + k
                    wpar = tau % 4
                    if tau >= 4:
                        load_wgwu(cg, tau)

                    def mm_G(e, wpar=wpar):
                        for dk in range(16):
                            for n in range(2):
                                ins = e.matmul(ps[1][:, 512 * n:512 * (n + 1)],
                                               wg_sb[wpar][:, dk, :],
                                               h2T[:, dk, 512 * n:512 * (n + 1)],
                                               start=(dk == 0), stop=(dk == 15))
                        return ins
                    p.op("t", mm_G, reads=[r_wg[wpar], r_h2T], writes=[res_psF[1]])

                    def mm_U(e, wpar=wpar):
                        for dk in range(16):
                            for n in range(2):
                                ins = e.matmul(ps[1][:, CHUNK + 512 * n:CHUNK + 512 * (n + 1)],
                                               wu_sb[wpar][:, dk, :],
                                               h2T[:, dk, 512 * n:512 * (n + 1)],
                                               start=(dk == 0), stop=(dk == 15))
                        return ins
                    p.op("t", mm_U, reads=[r_wu[wpar], r_h2T], writes=[res_psF[1]])
                    p.op("s", lambda e, k=k: e.activation(
                        hf[k][:], ps[1][:, 0:CHUNK], AF.Silu),
                        reads=[res_psF[1]], writes=[r_hf[k]])
                    p.op("v", lambda e: e.tensor_copy(Uf[:], ps[1][:, CHUNK:2 * CHUNK]),
                         reads=[res_psF[1]], writes=[r_Uf])
                    p.op("v", lambda e, k=k: e.tensor_tensor(af[k][:], hf[k][:], Uf[:], ALU.mult),
                         reads=[r_hf[k], r_Uf], writes=[r_af[k]])
                    p.op("s", lambda e, k=k: e.activation(hf[k][:], af[k][:], AF.Abs),
                         reads=[r_af[k]], writes=[r_hf[k]])
                    for bi_ in cmp_after.get(k, []):
                        a_, b_ = pairs[bi_]
                        p.op("v", lambda e, bi_=bi_, a_=a_, b_=b_: e.tensor_tensor(
                            bcmp[bi_][:], hf[a_][:], hf[b_][:], ALU.is_ge),
                            reads=[r_hf[a_], r_hf[b_]], writes=[r_b[bi_]])
                p.op("v", lambda e: e.tensor_tensor(msc[0][:], bcmp[0][:], bcmp[1][:], ALU.add),
                     reads=[r_b[0], r_b[1]], writes=[r_msc[0]])
                p.op("v", lambda e: e.tensor_tensor(msc[1][:], msc[0][:], bcmp[2][:], ALU.add),
                     reads=[r_msc[0], r_b[2]], writes=[r_msc[1]])
                p.op("v", lambda e: e.tensor_scalar(keep[0][:], msc[1][:], 1.5, None, ALU.is_ge),
                     reads=[r_msc[1]], writes=[r_k[0]])
                p.op("v", lambda e: e.tensor_tensor(msc[0][:], bcmp[3][:], bcmp[4][:], ALU.add),
                     reads=[r_b[3], r_b[4]], writes=[r_msc[0]])
                p.op("v", lambda e: e.tensor_tensor(msc[1][:], msc[0][:], bcmp[0][:], ALU.subtract),
                     reads=[r_msc[0], r_b[0]], writes=[r_msc[1]])
                p.op("v", lambda e: e.tensor_scalar(keep[1][:], msc[1][:], 0.5, None, ALU.is_ge),
                     reads=[r_msc[1]], writes=[r_k[1]])
                p.op("v", lambda e: e.tensor_tensor(msc[0][:], bcmp[1][:], bcmp[3][:], ALU.add),
                     reads=[r_b[1], r_b[3]], writes=[r_msc[0]])
                p.op("v", lambda e: e.tensor_tensor(msc[1][:], bcmp[5][:], msc[0][:], ALU.subtract),
                     reads=[r_msc[0], r_b[5]], writes=[r_msc[1]])
                p.op("v", lambda e: e.tensor_scalar(keep[2][:], msc[1][:], -0.5, None, ALU.is_ge),
                     reads=[r_msc[1]], writes=[r_k[2]])
                p.op("v", lambda e: e.tensor_tensor(msc[0][:], keep[0][:], keep[1][:], ALU.add),
                     reads=[r_k[0], r_k[1]], writes=[r_msc[0]])
                p.op("v", lambda e: e.tensor_tensor(msc[1][:], msc[0][:], keep[2][:], ALU.add),
                     reads=[r_msc[0], r_k[2]], writes=[r_msc[1]])
                p.op("v", lambda e: e.tensor_scalar(keep[3][:], msc[1][:], -1.0, 2.0, ALU.mult, ALU.add),
                     reads=[r_msc[1]], writes=[r_k[3]])
                for k in range(4):
                    tau = 4 * j + k
                    p.op("v", lambda e, k=k, tau=tau, hsl=hsl: e.tensor_tensor(
                        hsl[:, tau, :], af[k][:], keep[k][:], ALU.mult),
                        reads=[r_af[k], r_k[k]], writes=[r_hsl])

        def ph2(cg):
            hsl, r_hsl = hs2[cg % 2], r_hs2[cg % 2]
            for mi in range(8):
                ypar = mi % 2
                psY = ps[ypar]

                def mm_y(e, mi=mi, psY=psY, hsl=hsl):
                    for tau in range(8):
                        for n in range(4):
                            ins = e.matmul(psY[:, 512 * n:512 * (n + 1)],
                                           hsl[:, tau, 128 * mi:128 * (mi + 1)],
                                           wd_sb[:, tau, 512 * n:512 * (n + 1)],
                                           start=(tau == 0), stop=(tau == 7))
                    return ins
                p.op("t", mm_y, reads=[r_hsl, r_wd], writes=[res_psF[ypar]])
                p.op("v", lambda e, ypar=ypar, psY=psY: e.tensor_tensor(
                    ytile[ypar][:], psY[:, :], gbrep[:], ALU.mult),
                    reads=[res_psF[ypar], r_gbrep], writes=[r_yt[ypar]])
                row0 = CHUNK * ch + 128 * mi
                if not skip_ffn_accum:
                    p.dma("g", f"ac{mi}", lambda e, row0=row0, ypar=ypar: e.dma_start(
                        out=yout[row0:row0 + 128, :], in_=ytile[ypar][:],
                        accum_op=ALU.add),
                        reads=[r_yt[ypar]], writes=[r_out[row0 // 128]])

        for t_ in range(4):
            load_wgwu(0, t_)
        emit_wd(0)
        p.dma("g", "bcf", lambda e: e.dma_start(out=gbrep[:], in_=modscr[5:6, :].broadcast_to([128, D])),
              reads=[r_modscr], writes=[r_gbrep])
        ph1(0)
        for t_ in range(4):
            load_wgwu(1, t_)
        if dbg and ch == 0:
            for tau_ in range(8):
                p.dma("g", "dbg", lambda e, tau_=tau_: e.dma_start(
                    out=dbg_hs[tau_], in_=hs2[0][:, tau_, :]), reads=[r_hs2[0]])
        for cg in range(8):
            if cg < 7:
                ph1(cg + 1)
            if cg < 6:
                for t_ in range(4):
                    load_wgwu(cg + 2, t_)
            ph2(cg)
            if cg < 7:
                emit_wd(cg + 1)

    p.barrier()

    sems = {}
    from contextlib import ExitStack
    with ExitStack() as ctx:
        for name in list(p.cnt):
            sems[name] = ctx.enter_context(nc.semaphore(name))
        nc.all_engine_barrier()
        for name in list(p.cnt):
            nc.gpsimd.sem_clear(sems[name])
        nc.all_engine_barrier()
        p.replay(nc, sems)
    return nc


_CACHED = None


def _get_program():
    global _CACHED
    if _CACHED is None:
        _CACHED = build_program()
    return _CACHED


def kernel(L0_attn_x, L0_ffn_w_gate, L0_ffn_w_up, L0_ffn_w_down, L0_c, L0_wmod):
    x = np.ascontiguousarray(np.asarray(L0_attn_x, dtype=np.float32))
    wg = np.ascontiguousarray(np.asarray(L0_ffn_w_gate, dtype=np.float32))
    wu = np.ascontiguousarray(np.asarray(L0_ffn_w_up, dtype=np.float32))
    wd = np.ascontiguousarray(np.asarray(L0_ffn_w_down, dtype=np.float32))
    c = np.ascontiguousarray(np.asarray(L0_c, dtype=np.float32))
    wm = np.ascontiguousarray(np.asarray(L0_wmod, dtype=np.float32))

    wgP = np.ascontiguousarray(
        wg.reshape(16, 128, 8, 2, 128, 4).transpose(2, 3, 5, 1, 0, 4))
    wuP = np.ascontiguousarray(
        wu.reshape(16, 128, 8, 2, 128, 4).transpose(2, 3, 5, 1, 0, 4))
    wdP = np.ascontiguousarray(
        wd.reshape(8, 2, 128, 4, D).transpose(0, 1, 3, 2, 4))
    ident = np.eye(128, dtype=ml_dtypes.bfloat16)

    nc = _get_program()
    in_maps = []
    for core in range(NCORES):
        b, pp = core // 2, core % 2
        own = x[b, TOK * pp:TOK * (pp + 1), :]
        oth = x[b, TOK * (1 - pp):TOK * (2 - pp), :]
        xarr = np.ascontiguousarray(np.concatenate([own, oth], axis=0))
        in_maps.append(dict(xarr=xarr, crow=c[b:b + 1, :], wmod=wm,
                            wgP=wgP, wuP=wuP, wdP=wdP, ident=ident))
    res = run_bass_kernel_spmd(nc, in_maps, core_ids=list(range(NCORES)))
    out = np.empty((B, S, D), dtype=np.float32)
    for core in range(NCORES):
        b, pp = core // 2, core % 2
        out[b, TOK * pp:TOK * (pp + 1), :] = res.results[core]["yout"]
    return out
